# version 05
# baseline (speedup 1.0000x reference)
import numpy as np
import ml_dtypes
import concourse.bass as bass
import concourse.mybir as mybir
from concourse.bass_utils import run_bass_kernel_spmd

F32 = mybir.dt.float32
BF16 = mybir.dt.bfloat16
I32 = mybir.dt.int32
U8 = mybir.dt.uint8
AF = mybir.ActivationFunctionType
ALU = mybir.AluOpType
AX = mybir.AxisListType

B, L, E, NH, DH = 2, 8192, 1024, 16, 64
TOK = 2048
NT = TOK // 128
HID = 3072
NCORE = 8
EPS_LN = 1e-5
EPS_AT = 1e-6
SC1 = 1.0 / 127.0
SC2 = 1.0 / (127.0 * 127.0)


class Ctr:
    def __init__(self, stack, nc, name):
        self.h = stack.enter_context(nc.semaphore(name))
        self.n = 0


class Em:
    ENGINES = ("sync", "gpsimd", "scalar", "vector", "tensor")

    def __init__(self):
        self.ops = {k: [] for k in self.ENGINES}
        self.lastwait = {k: {} for k in self.ENGINES}

    def wait(self, eng, ctr, val=None):
        v = ctr.n if val is None else val
        if v is None or v <= 0:
            return
        key = id(ctr)
        if self.lastwait[eng].get(key, -1) >= v:
            return
        self.lastwait[eng][key] = v
        h = ctr.h
        self.ops[eng].append(lambda e: e.wait_ge(h, v))

    def dma(self, eng, out, in_, ctr, **kw):
        h = ctr.h
        self.ops[eng].append(lambda e: e.dma_start(out=out, in_=in_, **kw).then_inc(h, 16))
        ctr.n += 16
        return ctr.n

    def op(self, eng, meth, *args, inc=None, **kw):
        if inc is not None:
            h = inc.h
            inc.n += 1
            v = inc.n
            self.ops[eng].append(
                lambda e: getattr(e, meth)(*args, **kw).then_inc(h, 1))
            return v
        self.ops[eng].append(lambda e: getattr(e, meth)(*args, **kw))
        return None

    def raw(self, eng, fn):
        self.ops[eng].append(fn)


def build(debug=False, n_t=NT, phase_limit=3, b_sub=4):
    nc = bass.Bass("TRN2", target_bir_lowering=False, detect_race_conditions=False)
    okind = "ExternalOutput" if debug else "Internal"

    x_d = nc.dram_tensor("x", [n_t * 128, E], F32, kind="ExternalInput").ap()
    wqkv_d = nc.dram_tensor("wqkv", [E, 3 * E], F32, kind="ExternalInput").ap()
    bqkv_d = nc.dram_tensor("bqkv", [3, E], F32, kind="ExternalInput").ap()
    wout_d = nc.dram_tensor("wout", [E, E], F32, kind="ExternalInput").ap()
    bout_d = nc.dram_tensor("bout", [1, E], F32, kind="ExternalInput").ap()
    wg_d = nc.dram_tensor("wg", [HID, E], I32, kind="ExternalInput").ap()
    wu_d = nc.dram_tensor("wu", [HID, E], I32, kind="ExternalInput").ap()
    wd_d = nc.dram_tensor("wd", [E, HID], I32, kind="ExternalInput").ap()
    ident_d = nc.dram_tensor("ident", [128, 128], BF16, kind="ExternalInput").ap()
    mask4_d = nc.dram_tensor("mask4", [128, 512], BF16, kind="ExternalInput").ap()
    ones1_d = nc.dram_tensor("ones1", [128, 1], BF16, kind="ExternalInput").ap()
    ones2_d = nc.dram_tensor("ones2", [2, 128], BF16, kind="ExternalInput").ap()
    maskv_d = nc.dram_tensor("maskv", [64, 4], F32, kind="ExternalInput").ap()

    y_d = nc.dram_tensor("y", [n_t * 128, E], F32, kind="ExternalOutput").ap()

    wgbf_d = nc.dram_tensor("wgbf", [HID, E], BF16, kind="Internal").ap()
    wubf_d = nc.dram_tensor("wubf", [HID, E], BF16, kind="Internal").ap()
    wdbf_d = nc.dram_tensor("wdbf", [E, HID], BF16, kind="Internal").ap()
    b2q_d = nc.dram_tensor("b2q", [2, 3 * E], BF16, kind="Internal").ap()
    b2o_d = nc.dram_tensor("b2o", [2, E], BF16, kind="Internal").ap()
    xnf_d = nc.dram_tensor("xnf", [n_t // 2, 8, 128, 256], BF16, kind=okind).ap()
    qk_d = nc.dram_tensor("qfm", [n_t, 64, 4 * E], BF16, kind=okind).ap()
    ktv_d = nc.dram_tensor("ktv", [n_t, 128, 2 * E], BF16, kind=okind).ap()
    ccin_d = nc.dram_tensor("ccin", [64, 1040], F32, kind="Internal").ap()
    ccout_d = nc.dram_tensor("ccout", [256, 1040], F32, kind="Internal").ap()
    if debug:
        dbg_sloc = nc.dram_tensor("dbg_sloc", [64, 1040], F32, kind="ExternalOutput").ap()
        dbg_s0 = nc.dram_tensor("dbg_s0", [64, 1040], F32, kind="ExternalOutput").ap()
        dbg_s = nc.dram_tensor("dbg_s", [n_t, 128, E], BF16, kind="ExternalOutput").ap()
        dbg_g = nc.dram_tensor("dbg_g", [n_t, 128, E], BF16, kind="ExternalOutput").ap()
        dbg_spre = nc.dram_tensor("dbg_spre", [n_t, 128, E], BF16, kind="ExternalOutput").ap()
        dbg_nrm = nc.dram_tensor("dbg_nrm", [n_t, 128, 32], F32, kind="ExternalOutput").ap()

    WgT = nc.alloc_sbuf_tensor("WgT", [128, 8 * HID], BF16)
    WuT = nc.alloc_sbuf_tensor("WuT", [128, 8 * HID], BF16)
    R3 = nc.alloc_sbuf_tensor("R3", [128, 8 * HID], BF16)
    Wqkv = R3
    WdT = R3
    Wout = nc.alloc_sbuf_tensor("Wout", [128, 8 * E], BF16)
    identS = nc.alloc_sbuf_tensor("identS", [128, 128], BF16)
    mask4S = nc.alloc_sbuf_tensor("mask4S", [128, 512], BF16)
    ones1S = nc.alloc_sbuf_tensor("ones1S", [128, 1], BF16)
    ones2S = nc.alloc_sbuf_tensor("ones2S", [2, 128], BF16)
    maskvS = nc.alloc_sbuf_tensor("maskvS", [64, 4], F32)

    UN_BYTES = 47232
    un = nc.alloc_sbuf_tensor("UN", [128, UN_BYTES], U8)
    ub = nc.lookup_mloc(un).addr

    def at(name, shape, dtype, off):
        return nc.alloc_sbuf_tensor_at(name, shape, dtype, offset=ub + off)

    xt = [at("xt0", [128, E], F32, 0), at("xt1", [128, E], F32, 4096)]
    em_t = at("em", [128, 512], BF16, 8192)
    qbf = at("qbf", [128, E], BF16, 9216)
    ktvA = at("ktvA", [128, 2 * E], BF16, 11264)
    qfmA = at("qfmA", [128, E], BF16, 15360)
    kfmA = at("kfmA", [128, E], BF16, 17408)
    workA = at("workA", [128, E], F32, 19456)
    etmp = at("etmp", [128, 512], F32, 19456)
    xnbf = at("xnbf", [128, E], BF16, 23552)
    xnfmA = at("xnfmA", [128, E], BF16, 25600)
    sloc = [at("sloc0", [64, 1040], F32, 27648), at("sloc1", [64, 1040], F32, 31872)]
    bias2q = at("bias2q", [2, 3 * E], BF16, 36096)
    sums = [at(f"sums{i}", [128, 1], F32, 42240 + 384 * i) for i in range(2)]
    negm = [at(f"negm{i}", [128, 1], F32, 42304 + 384 * i) for i in range(2)]
    ssq = [at(f"ssq{i}", [128, 1], F32, 42368 + 384 * i) for i in range(2)]
    lnv = [at(f"lnv{i}", [128, 1], F32, 42432 + 384 * i) for i in range(2)]
    rstd = [at(f"rstd{i}", [128, 1], F32, 42496 + 384 * i) for i in range(2)]
    mbias = [at(f"mbias{i}", [128, 1], F32, 42560 + 384 * i) for i in range(2)]
    stg = [at("stg0", [128, E], BF16, 43008), at("stg1", [128, E], BF16, 45056)]
    bstq = at("bstq", [3, E], F32, 0)
    bsto = at("bsto", [1, E], F32, 4096)
    bhiq = at("bhiq", [3, E], BF16, 8192)
    bloq = at("bloq", [3, E], BF16, 11264)
    bhio = at("bhio", [1, E], BF16, 15360)
    bloo = at("bloo", [1, E], BF16, 17408)

    S_run = [at("Srun0", [64, 1024], F32, 0), at("Srun1", [64, 1024], F32, 4096)]
    S_bf = at("Sbf", [64, 1024], BF16, 8192)
    z_run = [at("zrun0", [64, 16], F32, 10240), at("zrun1", [64, 16], F32, 10304)]
    z_bf = at("zbf", [64, 16], BF16, 10368)
    norm_eps = at("normeps", [128, 16], F32, 10432)
    recip = at("recip", [128, 16], F32, 10496)
    qkM = at("qkM", [64, 4 * E], BF16, 10624)
    ktvM = at("ktvM", [128, 2 * E], BF16, 18816)
    at_bf2 = [at("atbf0", [128, 512], BF16, 22912), at("atbf1", [128, 512], BF16, 37248)]
    s_pre = at("spre", [128, E], BF16, 23936)
    sfm = at("sfm", [128, E], BF16, 23936)
    s_sup = at("ssup", [128, 2 * E], BF16, 25984)
    xnfM = at("xnfM", [128, 2048], BF16, 30080)
    hp = at("hp", [128, 1024], BF16, 34176)
    silu_t = [at("silu0", [128, 256], F32, 36224), at("silu1", [128, 256], F32, 36224)]
    g_bf = at("gbf", [128, E], BF16, 38272)
    out_sb = [at("outsb0", [128, 512], F32, 40320), at("outsb1", [128, 512], F32, 42368)]
    bias2o = at("bias2o", [2, E], BF16, 44416)
    gb = [at("gb0", [64, 1040], F32, 23936), at("gb1", [64, 1040], F32, 32256)]
    gterm = at("gterm", [64, 1040], F32, 38272)
    gacc = [at("gacc0", [64, 1040], F32, 42432), at("gacc1", [64, 1040], F32, 28096)]

    P0 = nc.alloc_psum_tensor("P0", [128, 512], F32)
    P1 = nc.alloc_psum_tensor("P1", [128, 512], F32)
    P23 = nc.alloc_psum_tensor("P23", [128, 1024], F32)
    P4 = nc.alloc_psum_tensor("P4", [128, 512], F32)
    P56 = nc.alloc_psum_tensor("P56", [128, 1024], F32)
    P7 = nc.alloc_psum_tensor("P7", [128, 512], F32)
    P0b = P0.ap().bitcast(BF16)
    P1b = P1.ap().bitcast(BF16)
    P23b = P23.ap().bitcast(BF16)

    from contextlib import ExitStack
    stack = ExitStack()
    sSY = Ctr(stack, nc, "sSY")
    sGP = Ctr(stack, nc, "sGP")
    sPE = Ctr(stack, nc, "sPE")
    sDV = Ctr(stack, nc, "sDV")
    sAC = Ctr(stack, nc, "sAC")
    sCC = Ctr(stack, nc, "sCC")

    em = Em()
    M = {}

    em.dma("sync", identS[:, :], ident_d, sSY)
    em.dma("sync", mask4S[:, :], mask4_d, sSY)
    em.dma("sync", ones1S[:, :], ones1_d, sSY)
    em.dma("sync", ones2S[:, :], ones2_d, sSY)
    em.dma("sync", maskvS[:, :], maskv_d, sSY)
    em.dma("sync", bstq[:, :], bqkv_d, sSY)
    em.dma("sync", bsto[:, :], bout_d, sSY)
    M["consts"] = sSY.n

    wq3 = wqkv_d.rearrange("(c p) n -> c p n", p=128)
    for c in range(8):
        em.dma("gpsimd", Wqkv[:, c * 3 * E:(c + 1) * 3 * E], wq3[c], sGP)
    M["wqkv"] = sGP.n
    wo3 = wout_d.rearrange("(c p) n -> c p n", p=128)
    for c in range(8):
        em.dma("gpsimd", Wout[:, c * E:(c + 1) * E], wo3[c], sGP)
    M["wout"] = sGP.n

    em.wait("vector", sSY, M["consts"])
    _bq = em.op("vector", "tensor_copy", bhiq[:, :], bstq[:, :], inc=sDV)
    em.wait("vector", sDV, _bq)
    em.op("vector", "tensor_tensor", bloq[:, :], bstq[:, :], bhiq[:, :], op=ALU.subtract)
    _bo = em.op("vector", "tensor_copy", bhio[:, :], bsto[:, :], inc=sDV)
    em.wait("vector", sDV, _bo)
    M["biasdv"] = em.op("vector", "tensor_tensor", bloo[:, :], bsto[:, :], bhio[:, :],
                        op=ALU.subtract, inc=sDV)
    em.wait("gpsimd", sDV, M["biasdv"])
    em.dma("gpsimd", b2q_d[0:1, :].rearrange("a (b c) -> (a b) c", b=3), bhiq[:, :], sGP)
    em.dma("gpsimd", b2q_d[1:2, :].rearrange("a (b c) -> (a b) c", b=3), bloq[:, :], sGP)
    em.dma("gpsimd", b2o_d[0:1, :], bhio[:, :], sGP)
    em.dma("gpsimd", b2o_d[1:2, :], bloo[:, :], sGP)
    M["biasbounce"] = sGP.n
    em.wait("sync", sGP, M["biasbounce"])
    em.dma("sync", bias2q[:, :], b2q_d, sSY)
    M["bias2"] = sSY.n

    def load_act_table(set_name):
        def f(e):
            from concourse.hw_specs import get_activation_tables
            names = list(get_activation_tables(nc.m.arch).keys())
            e.add_instruction(mybir.InstLoadActFuncSet(
                name=nc.get_next_instruction_name(),
                act_func_set_id=names.index(set_name), ins=[], outs=[]))
        em.raw("scalar", f)

    load_act_table("natural_log_exp_and_others")

    bounce_jobs = []
    for src, dst in ((wg_d, wgbf_d), (wu_d, wubf_d), (wd_d, wdbf_d)):
        s4 = src.rearrange("(t p) (c d) -> t c p d", p=128, d=1024)
        d4 = dst.rearrange("(t p) (c d) -> t c p d", p=128, d=1024)
        for t in range(s4.shape[0]):
            for c in range(s4.shape[1]):
                bounce_jobs.append((s4[t, c], d4[t, c]))
    bstate = {"i": 0, "hist": []}

    def emit_bounce(n):
        for _ in range(n):
            i = bstate["i"]
            if i >= len(bounce_jobs):
                return
            src, dst = bounce_jobs[i]
            b = stg[i % 2]
            if i >= 2:
                em.wait("gpsimd", sGP, bstate["hist"][i - 2][1])
            ldv = em.dma("gpsimd", b[:, :], src, sGP)
            em.wait("gpsimd", sGP, ldv)
            stv = em.dma("gpsimd", dst, b[:, :], sGP)
            bstate["hist"].append((ldv, stv))
            bstate["i"] += 1

    x3 = x_d.rearrange("(t p) d -> t p d", p=128)
    y3 = y_d.rearrange("(t p) d -> t p d", p=128)

    if phase_limit < 1:
        em.wait("gpsimd", sGP, sGP.n)
        em.wait("sync", sSY, sSY.n)
        return _finish(nc, em)

    h = [dict() for _ in range(n_t)]

    def emit_xload(t):
        ht = h[t]
        if t == 0:
            em.wait("sync", sDV, M["biasdv"])
        if t >= 2:
            em.wait("sync", sAC, h[t - 2]["xnbf"])
        ht["x"] = em.dma("sync", xt[t % 2][:, :], x3[t], sSY)

    def emit_ln(t):
        ht = h[t]
        p2 = t % 2
        em.wait("vector", sSY, ht["x"])
        if t == 0:
            em.wait("vector", sGP, M["biasbounce"])
        _rv = em.op("vector", "reduce_sum", sums[p2][:, :], xt[t % 2][:, :],
                    axis=AX.X, inc=sDV)
        em.wait("vector", sDV, _rv)
        ht["negm"] = em.op("vector", "tensor_scalar", negm[p2][:, :], sums[p2][:, :],
                           -1.0 / E, None, op0=ALU.mult, inc=sDV)
        em.wait("scalar", sSY, ht["x"])
        em.wait("scalar", sDV, ht["negm"])
        if t >= 1:
            em.wait("scalar", sPE, h[t - 1]["tr_xn"])
        sqv = em.op("scalar", "activation", xnbf[:, :], xt[t % 2][:, :], AF.Square,
                    bias=negm[p2][:, :], accum_out=ssq[p2][:, :], inc=sAC)
        em.wait("vector", sAC, sqv)
        veps = em.op("vector", "tensor_scalar", lnv[p2][:, :], ssq[p2][:, :], 1.0 / E,
                     EPS_LN, op0=ALU.mult, op1=ALU.add, inc=sDV)
        em.wait("scalar", sDV, veps)
        em.op("scalar", "activation", lnv[p2][:, :], lnv[p2][:, :], AF.Ln)
        ht["rstd"] = em.op("scalar", "activation", rstd[p2][:, :], lnv[p2][:, :],
                           AF.Exp, scale=-0.5, inc=sAC)
        em.wait("vector", sAC, ht["rstd"])
        em.wait("vector", sDV, ht["negm"])
        ht["mb"] = em.op("vector", "tensor_scalar", mbias[p2][:, :], negm[p2][:, :],
                         rstd[p2][:, :], None, op0=ALU.mult, inc=sDV)
        em.wait("scalar", sDV, ht["mb"])
        ht["xnbf"] = em.op("scalar", "activation", xnbf[:, :], xt[t % 2][:, :],
                           AF.Identity, bias=mbias[p2][:, :], scale=rstd[p2][:, :],
                           inc=sAC)

    def emit_trxn(t):
        ht = h[t]
        em.wait("tensor", sSY, M["consts"])
        em.wait("tensor", sAC, ht["xnbf"])
        if t >= 1:
            em.wait("tensor", sDV, h[t - 1]["kfm_ev"])
        for j in range(8):
            kw = dict(inc=sPE) if j == 7 else {}
            r = em.op("tensor", "transpose", P23b[:, 128 * j:128 * (j + 1)],
                      xnbf[:, 128 * j:128 * (j + 1)], identS[:, :], **kw)
        ht["tr_xn"] = r
        em.wait("vector", sPE, ht["tr_xn"])
        if t >= 1:
            em.wait("vector", sGP, h[t - 1]["st_xnf"])
            em.wait("vector", sPE, h[t - 1]["qkv_done"])
        ht["xnfm_ev"] = em.op("vector", "tensor_copy", xnfmA[:, :], P23b[:, 0:1024],
                              inc=sDV)
        em.wait("gpsimd", sDV, ht["xnfm_ev"])
        dst = xnf_d[t // 2][:, :, (t % 2) * 128:(t % 2) * 128 + 128].rearrange(
            "k p u -> p k u")
        ht["st_xnf"] = em.dma("gpsimd", dst,
                              xnfmA[:, :].rearrange("p (k u) -> p k u", k=8), sGP)

    def emit_qkv(t):
        ht = h[t]
        em.wait("tensor", sGP, M["wqkv"])
        em.wait("tensor", sSY, M["bias2"])
        em.wait("tensor", sDV, ht["xnfm_ev"])
        ht["elu"] = [None] * 6
        ht["qkv_cc"] = []
        for cc in range(6):
            pb = (P0, P1, P7)[cc % 3]
            if cc >= 3:
                em.wait("tensor", sDV, ht["elu"][cc - 3])
            elif t >= 1:
                em.wait("tensor", sDV, h[t - 1]["elu"][cc + 3])
            for kc in range(8):
                em.op("tensor", "matmul", pb[:, :],
                      xnfmA[:, 128 * kc:128 * (kc + 1)],
                      Wqkv[:, 3 * E * kc + 512 * cc:3 * E * kc + 512 * (cc + 1)],
                      start=(kc == 0), stop=False)
            v = em.op("tensor", "matmul", pb[:, :], ones2S[:, :],
                      bias2q[:, 512 * cc:512 * (cc + 1)], start=False, stop=True,
                      inc=sPE)
            ht["qkv_cc"].append(v)
            if cc < 4:
                em.wait("scalar", sPE, v)
                if cc >= 1:
                    em.wait("scalar", sDV, ht["elu"][cc - 1])
                elif t >= 1:
                    em.wait("scalar", sDV, h[t - 1]["elu"][3])
                ev = em.op("scalar", "activation", etmp[:, :], pb[:, :], AF.Exp,
                           inc=sAC)
                em.wait("vector", sAC, ev)
                _mv = em.op("vector", "tensor_scalar", em_t[:, :], etmp[:, :], 1.0,
                            None, op0=ALU.min, inc=sDV)
                em.wait("vector", sDV, _mv)
                if cc < 2:
                    dest = qbf[:, 512 * cc:512 * (cc + 1)]
                    if t >= 1:
                        em.wait("vector", sPE, h[t - 1]["tr_q"])
                else:
                    dest = ktvA[:, 512 * (cc - 2):512 * (cc - 1)]
                    if t >= 1:
                        em.wait("vector", sPE, h[t - 1]["ds"])
                        em.wait("vector", sGP, h[t - 1]["st_ktv"])
                ht["elu"][cc] = em.op("vector", "scalar_tensor_tensor", dest,
                                      pb[:, :], 0.0, em_t[:, :],
                                      op0=ALU.max, op1=ALU.add, inc=sDV)
            else:
                em.wait("vector", sPE, v)
                if t >= 1:
                    em.wait("vector", sPE, h[t - 1]["ds"])
                    em.wait("vector", sGP, h[t - 1]["st_ktv"])
                dest = ktvA[:, 1024 + 512 * (cc - 4):1024 + 512 * (cc - 3)]
                ht["elu"][cc] = em.op("vector", "tensor_copy", dest, pb[:, :], inc=sDV)
        ht["qkv_done"] = ht["qkv_cc"][5]

    def emit_trqk(t):
        ht = h[t]
        em.wait("tensor", sDV, ht["elu"][1])
        if t >= 1:
            em.wait("tensor", sDV, h[t - 1]["qfm_ev"])
        for j in range(8):
            kw = dict(inc=sPE) if j == 7 else {}
            r = em.op("tensor", "transpose",
                      P23b[:, 1024 + 128 * j:1024 + 128 * (j + 1)],
                      qbf[:, 128 * j:128 * (j + 1)], identS[:, :], **kw)
        ht["tr_q"] = r
        em.wait("tensor", sDV, ht["elu"][3])
        em.wait("tensor", sDV, ht["xnfm_ev"])
        for j in range(8):
            kw = dict(inc=sPE) if j == 7 else {}
            r = em.op("tensor", "transpose", P23b[:, 128 * j:128 * (j + 1)],
                      ktvA[:, 128 * j:128 * (j + 1)], identS[:, :], **kw)
        ht["tr_k"] = r
        em.wait("vector", sPE, ht["tr_q"])
        if t >= 1:
            em.wait("vector", sGP, h[t - 1]["st_qfm"])
        ht["qfm_ev"] = em.op("vector", "tensor_copy", qfmA[:, :], P23b[:, 1024:2048],
                             inc=sDV)
        em.wait("vector", sPE, ht["tr_k"])
        if t >= 1:
            em.wait("vector", sGP, h[t - 1]["st_kfm"])
        ht["kfm_ev"] = em.op("vector", "tensor_copy", kfmA[:, :], P23b[:, 0:1024],
                             inc=sDV)
        em.wait("gpsimd", sDV, ht["qfm_ev"])
        qv3 = qk_d[t][:, 0:2048].rearrange("p (hh u) -> p hh u", u=128)
        em.dma("gpsimd", qv3[:, 0::2, :],
               qfmA[0:64, :].rearrange("p (j u) -> p j u", u=128), sGP)
        ht["st_qfm"] = em.dma("gpsimd", qv3[:, 1::2, :],
                              qfmA[64:128, :].rearrange("p (j u) -> p j u", u=128), sGP)
        em.wait("gpsimd", sDV, ht["kfm_ev"])
        kv3 = qk_d[t][:, 2048:4096].rearrange("p (hh u) -> p hh u", u=128)
        em.dma("gpsimd", kv3[:, 0::2, :],
               kfmA[0:64, :].rearrange("p (j u) -> p j u", u=128), sGP)
        ht["st_kfm"] = em.dma("gpsimd", kv3[:, 1::2, :],
                              kfmA[64:128, :].rearrange("p (j u) -> p j u", u=128), sGP)
        em.wait("gpsimd", sDV, ht["elu"][5])
        ht["st_ktv"] = em.dma("gpsimd", ktv_d[t], ktvA[:, :], sGP)

    def emit_ds(t):
        ht = h[t]
        em.wait("tensor", sDV, ht["elu"][3])
        em.wait("tensor", sDV, ht["elu"][5])
        if t >= 1:
            em.wait("tensor", sDV, h[t - 1]["sloc"])
        for hh in range(NH):
            em.op("tensor", "matmul", P56[0:64, 64 * hh:64 * (hh + 1)],
                  ktvA[:, 64 * hh:64 * (hh + 1)],
                  ktvA[:, 1024 + 64 * hh:1024 + 64 * (hh + 1)], start=True, stop=True)
            kw = dict(inc=sPE) if hh == NH - 1 else {}
            r = em.op("tensor", "matmul", P4[0:64, 16 + hh:17 + hh],
                      ktvA[:, 64 * hh:64 * (hh + 1)], ones1S[:, :],
                      start=True, stop=True, **kw)
        ht["ds"] = r

    def emit_sloc(t):
        ht = h[t]
        em.wait("vector", sPE, ht["ds"])
        cur, prv = sloc[t % 2], sloc[(t + 1) % 2]
        if t == 0:
            em.op("vector", "tensor_copy", cur[:, 0:1024], P56[0:64, :], inc=sDV)
            r = em.op("vector", "tensor_copy", cur[:, 1024:1040], P4[0:64, 16:32],
                      inc=sDV)
        else:
            em.wait("vector", sDV, h[t - 1]["sloc"])
            em.op("vector", "tensor_add", cur[:, 0:1024], P56[0:64, :],
                  prv[:, 0:1024], inc=sDV)
            r = em.op("vector", "tensor_add", cur[:, 1024:1040], P4[0:64, 16:32],
                      prv[:, 1024:1040], inc=sDV)
        ht["sloc"] = r

    emit_xload(0)
    emit_ln(0)
    for t in range(n_t):
        if t + 1 < n_t:
            emit_xload(t + 1)
        emit_trxn(t)
        if t >= 1:
            emit_ds(t - 1)
            emit_sloc(t - 1)
        if t + 1 < n_t:
            emit_ln(t + 1)
        emit_qkv(t)
        emit_trqk(t)
        emit_bounce(5)
    emit_ds(n_t - 1)
    emit_sloc(n_t - 1)

    emit_bounce(len(bounce_jobs))
    M["bounce_wg"] = bstate["hist"][23][1]
    M["bounce_wu"] = bstate["hist"][47][1]
    M["bounce_wd"] = bstate["hist"][71][1]

    if phase_limit < 2:
        em.wait("gpsimd", sGP, sGP.n)
        return _finish(nc, em)
    em.wait("gpsimd", sDV, h[n_t - 1]["sloc"])
    slocF = sloc[(n_t - 1) % 2]
    ccin_st = em.dma("gpsimd", ccin_d, slocF[:, :], sGP)
    em.wait("gpsimd", sGP, ccin_st)
    ccv = em.op("gpsimd", "collective_compute", "AllGather", ALU.bypass,
                replica_groups=[[0, 1, 2, 3], [4, 5, 6, 7]],
                ins=[ccin_d.opt()], outs=[ccout_d.opt()], inc=sCC)
    if debug:
        em.dma("gpsimd", dbg_sloc, ccin_d, sGP)

    em.wait("scalar", sGP, M["bounce_wg"])
    for dc in range(8):
        em.dma("scalar", WgT[:, HID * dc:HID * (dc + 1)],
               wgbf_d[:, 128 * dc:128 * (dc + 1)], sAC, transpose=True)
    em.wait("scalar", sGP, M["bounce_wu"])
    for dc in range(8):
        em.dma("scalar", WuT[:, HID * dc:HID * (dc + 1)],
               wubf_d[:, 128 * dc:128 * (dc + 1)], sAC, transpose=True)
    M["wgut"] = sAC.n
    em.wait("scalar", sPE, h[n_t - 1]["qkv_done"])
    em.wait("scalar", sGP, M["bounce_wd"])
    for hc in range(24):
        em.dma("scalar", WdT[:, E * hc:E * (hc + 1)],
               wdbf_d[:, 128 * hc:128 * (hc + 1)], sAC, transpose=True)
    M["wdt"] = sAC.n
    cco = ccout_d.rearrange("(r p) f -> r p f", p=64)
    prevmul = None
    for r in range(4):
        em.wait("sync", sCC, ccv)
        if r >= 2:
            em.wait("sync", sDV, M[f"gmul{r - 2}"])
        gl = em.dma("sync", gb[r % 2][:, :], cco[r], sSY)
        em.wait("vector", sSY, gl)
        if r == 0:
            M["gmul0"] = em.op("vector", "tensor_scalar", gacc[0][:, :], gb[0][:, :],
                               maskvS[:, 0:1], None, op0=ALU.mult, inc=sDV)
            M["gacc0"] = M["gmul0"]
        else:
            M[f"gmul{r}"] = em.op("vector", "tensor_scalar", gterm[:, :],
                                  gb[r % 2][:, :], maskvS[:, r:r + 1], None,
                                  op0=ALU.mult, inc=sDV)
            em.wait("vector", sDV, M[f"gmul{r}"])
            em.wait("vector", sDV, M[f"gacc{r - 1}"])
            M[f"gacc{r}"] = em.op("vector", "tensor_add", gacc[r % 2][:, :],
                                  gterm[:, :], gacc[(r - 1) % 2][:, :], inc=sDV)
    s0 = gacc[1]
    if debug:
        M["s0pre"] = em.op("vector", "engine_nop", inc=sDV)
        em.wait("gpsimd", sDV, M["s0pre"])
        em.dma("gpsimd", dbg_s0, s0[:, :], sGP)
    em.wait("vector", sDV, M["gacc3"])
    _c1 = em.op("vector", "tensor_copy", S_run[0][:, :], s0[:, 0:1024], inc=sDV)
    _c2 = em.op("vector", "tensor_copy", z_run[0][:, :], s0[:, 1024:1040], inc=sDV)
    em.wait("vector", sDV, _c2)
    em.op("vector", "tensor_copy", S_bf[:, :], S_run[0][:, :])
    M["sbf0"] = em.op("vector", "tensor_copy", z_bf[:, :], z_run[0][:, :], inc=sDV)
    M["gapdv"] = M["sbf0"]
    em.wait("sync", sGP, M["bounce_wd"])
    em.wait("sync", sDV, M["gapdv"])
    em.dma("sync", bias2o[:, :], b2o_d, sSY)
    M["bias2o"] = sSY.n
    M["allst"] = sGP.n

    load_act_table("silu_and_others")

    if phase_limit < 3:
        em.wait("gpsimd", sGP, sGP.n)
        em.wait("vector", sDV, sDV.n)
        return _finish(nc, em)

    def head_blk(tens, hh, w):
        return tens[0:64, w * hh:w * (hh + 1)]

    g = [dict() for _ in range(n_t)]

    for t in range(n_t):
        gt = g[t]
        sup = t // 2
        em.wait("sync", sGP, M["allst"])
        em.wait("gpsimd", sGP, M["allst"])
        if t >= 1:
            em.wait("sync", sPE, g[t - 1]["attn_mm"])
            em.wait("gpsimd", sPE, g[t - 1]["attn_mm"])
        gt["ld_q"] = em.dma("sync", qkM[:, :], qk_d[t], sSY)
        gt["ld_k"] = gt["ld_q"]
        gt["ld_v"] = em.dma("gpsimd", ktvM[:, :], ktv_d[t], sGP)
        if t % 2 == 0:
            em.wait("sync", sDV, M["gapdv"])
            if sup >= 1:
                em.wait("sync", sPE, g[t - 1]["ffn_gu"])
            gt["ld_xnf"] = em.dma("sync", xnfM[:, :].rearrange("p (k u) -> p k u", k=8),
                                  xnf_d[sup].rearrange("k p u -> p k u"), sSY)

        if b_sub == 11:
            gt["attn_mm"] = 0
            gt["sev"] = 0; gt["sbf_sup"] = 0; gt["normread"] = 0
            gt["supd"] = 0; gt["sbf"] = 0; gt["heads"] = [0, 0, 0, 0]
            continue
        em.wait("tensor", sSY, gt["ld_q"])
        gt["atp"] = []
        gt["mask"] = []
        gt["heads"] = []

        def at_pass(p):
            pb = (P0, P1)[p % 2]
            if p >= 2:
                em.wait("tensor", sDV, gt["mask"][p - 2])
            elif t >= 1:
                em.wait("tensor", sDV, g[t - 1]["sev"])
                if "ffn_silu_last" in g[t - 1]:
                    em.wait("tensor", sAC, g[t - 1]["ffn_silu_last"])
            for hi in range(4):
                hh = 4 * p + hi
                kw = dict(inc=sPE) if hi == 3 else {}
                r = em.op("tensor", "matmul", pb[:, 128 * hi:128 * (hi + 1)],
                          qkM[0:64, 2048 + 128 * hh:2176 + 128 * hh], qkM[0:64, 128 * hh:128 * (hh + 1)],
                          start=True, stop=True, **kw)
            gt["atp"].append(r)

        def mask_pass(p):
            pb = (P0, P1)[p % 2]
            ab = at_bf2[p % 2]
            em.wait("vector", sPE, gt["atp"][p])
            if p >= 2:
                em.wait("vector", sPE, gt["heads"][p - 2])
            elif t >= 1:
                em.wait("vector", sPE, g[t - 1]["heads"][2 + p])
            mk = em.op("vector", "tensor_tensor", ab[:, :], pb[:, :], mask4S[:, :],
                       op=ALU.mult, inc=sDV)
            gt["mask"].append(mk)

        def head_pass(p):
            em.wait("tensor", sDV, gt["mask"][p])
            em.wait("tensor", sDV, M["sbf0"] if t == 0 else g[t - 1]["sbf"])
            em.wait("tensor", sGP, gt["ld_v"])
            if p == 0:
                if t >= 1:
                    em.wait("tensor", sDV, g[t - 1]["sbf_sup"])
                    em.wait("tensor", sDV, g[t - 1]["normread"])
                    em.wait("tensor", sDV, g[t - 1]["supd"])
                if sup >= 1 and t % 2 == 0:
                    em.wait("tensor", sAC, g[t - 2]["g_mark"])
                    em.wait("tensor", sAC, g[t - 1]["g_mark"])
                    em.wait("tensor", sDV, g[t - 1]["ffn_mul_last"])
            ab = at_bf2[p % 2]
            for hi in range(4):
                hh = 4 * p + hi
                em.op("tensor", "matmul", P23[:, 64 * hh:64 * (hh + 1)],
                      ab[:, 128 * hi:128 * (hi + 1)],
                      ktvM[:, 1024 + 64 * hh:1024 + 64 * (hh + 1)],
                      start=True, stop=False)
                em.op("tensor", "matmul", P23[:, 64 * hh:64 * (hh + 1)],
                      qkM[0:64, 128 * hh:128 * (hh + 1)], head_blk(S_bf, hh, 64),
                      start=False, stop=True)
                em.op("tensor", "matmul", P4[:, hh:hh + 1],
                      ab[:, 128 * hi:128 * (hi + 1)], ones1S[:, :],
                      start=True, stop=False)
                em.op("tensor", "matmul", P4[:, hh:hh + 1],
                      qkM[0:64, 128 * hh:128 * (hh + 1)], head_blk(z_bf, hh, 1),
                      start=False, stop=True)
                em.op("tensor", "matmul", P56[0:64, 64 * hh:64 * (hh + 1)],
                      ktvM[:, 64 * hh:64 * (hh + 1)],
                      ktvM[:, 1024 + 64 * hh:1024 + 64 * (hh + 1)],
                      start=True, stop=True)
                kw = dict(inc=sPE) if hi == 3 else {}
                r = em.op("tensor", "matmul", P4[0:64, 16 + hh:17 + hh],
                          ktvM[:, 64 * hh:64 * (hh + 1)], ones1S[:, :],
                          start=True, stop=True, **kw)
            gt["heads"].append(r)

        if b_sub == 12:
            for p in range(4):
                at_pass(p)
                mask_pass(p)
                gt["heads"].append(gt["atp"][p])
            gt["attn_mm"] = gt["atp"][3]
            gt["sev"] = gt["mask"][3]; gt["sbf_sup"] = 0; gt["normread"] = 0
            gt["supd"] = 0; gt["sbf"] = gt["mask"][3]
            continue
        at_pass(0)
        mask_pass(0)
        at_pass(1)
        head_pass(0)
        mask_pass(1)
        at_pass(2)
        head_pass(1)
        mask_pass(2)
        at_pass(3)
        head_pass(2)
        mask_pass(3)
        head_pass(3)
        gt["attn_mm"] = gt["heads"][3]
        if b_sub == 13:
            gt["sev"] = gt["mask"][3]; gt["sbf_sup"] = 0; gt["normread"] = 0
            gt["supd"] = 0; gt["sbf"] = gt["mask"][3]
            continue

        em.wait("vector", sPE, gt["attn_mm"])
        cur, nxt = t % 2, (t + 1) % 2
        _ne = em.op("vector", "tensor_scalar", norm_eps[:, :], P4[:, 0:16], EPS_AT,
                    None, op0=ALU.add, inc=sDV)
        if t >= 1:
            em.wait("vector", sDV, g[t - 1]["supd_s"])
        gt["supd_s"] = em.op("vector", "tensor_add", S_run[nxt][:, :], P56[0:64, :],
                             S_run[cur][:, :], inc=sDV)
        gt["supd"] = em.op("vector", "tensor_add", z_run[nxt][:, :], P4[0:64, 16:32],
                           z_run[cur][:, :], inc=sDV)
        em.wait("vector", sDV, _ne)
        _rc = em.op("vector", "reciprocal", recip[:, :], norm_eps[:, :], inc=sDV)
        em.wait("vector", sDV, gt["supd"])
        em.op("vector", "tensor_copy", S_bf[:, :], S_run[nxt][:, :])
        gt["sbf"] = em.op("vector", "tensor_copy", z_bf[:, :], z_run[nxt][:, :], inc=sDV)
        if debug and t >= 1 and "dbg_spre" in g[t - 1]:
            em.wait("vector", sGP, g[t - 1]["dbg_spre"])
        em.wait("vector", sDV, _rc)
        gt["normread"] = em.op("vector", "tensor_tensor",
                               s_pre[:, :].rearrange("p (hh d) -> p hh d", hh=16),
                               P23.ap().rearrange("p (hh d) -> p hh d", hh=16),
                               recip[:, :].unsqueeze(2).broadcast_to([128, 16, 64]),
                               op=ALU.mult, inc=sDV)

        if b_sub < 2:
            gt["sev"] = gt["normread"]
            gt["sbf_sup"] = 0
            gt["outproj"] = 0
            continue
        if debug:
            em.wait("gpsimd", sDV, gt["normread"])
            gt["dbg_spre"] = em.dma("gpsimd", dbg_spre[t], s_pre[:, :], sGP)
            em.dma("gpsimd", dbg_nrm[t][:, 0:16], norm_eps[:, :], sGP)
            gt["dbg_nrm"] = em.dma("gpsimd", dbg_nrm[t][:, 16:32], recip[:, :], sGP)
        em.wait("tensor", sDV, gt["normread"])
        for j in range(8):
            dst = P0b[:, 128 * j:128 * (j + 1)] if j < 4 else \
                P1b[:, 128 * (j - 4):128 * (j - 3)]
            kw = dict(inc=sPE) if j == 7 else {}
            r = em.op("tensor", "transpose", dst, s_pre[:, 128 * j:128 * (j + 1)],
                      identS[:, :], **kw)
        gt["tr_s"] = r
        em.wait("vector", sPE, gt["tr_s"])
        if debug and "dbg_spre" in gt:
            em.wait("vector", sGP, gt["dbg_spre"])
        em.op("vector", "tensor_copy", sfm[:, 0:512], P0b[:, 0:512])
        gt["sev"] = em.op("vector", "tensor_copy", sfm[:, 512:1024], P1b[:, 0:512],
                          inc=sDV)
        em.wait("tensor", sDV, gt["sev"])
        em.wait("tensor", sGP, M["wout"])
        em.wait("tensor", sSY, M["bias2o"])
        for ncol in range(2):
            for kc in range(8):
                em.op("tensor", "matmul", P23[:, 512 * ncol:512 * (ncol + 1)],
                      sfm[:, 128 * kc:128 * (kc + 1)],
                      Wout[:, E * kc + 512 * ncol:E * kc + 512 * (ncol + 1)],
                      start=(kc == 0), stop=False)
            r = em.op("tensor", "matmul", P23[:, 512 * ncol:512 * (ncol + 1)],
                      ones2S[:, :], bias2o[:, 512 * ncol:512 * (ncol + 1)],
                      start=False, stop=True, inc=sPE)
        gt["outproj"] = r
        em.wait("vector", sPE, gt["outproj"])
        gt["sbf_sup"] = em.op("vector", "tensor_copy",
                              s_sup[:, 1024 * (t % 2):1024 * (t % 2 + 1)],
                              P23.ap(), inc=sDV)
        if debug:
            em.wait("gpsimd", sDV, gt["sbf_sup"])
            em.dma("gpsimd", dbg_s[t], s_sup[:, 1024 * (t % 2):1024 * (t % 2 + 1)], sGP)

        if t % 2 == 1 and b_sub >= 3:
            t0, t1 = 2 * sup, 2 * sup + 1
            em.wait("tensor", sAC, M["wgut"])
            em.wait("tensor", sSY, g[t0]["ld_xnf"])
            em.wait("tensor", sDV, gt["sbf_sup"])
            em.wait("tensor", sDV, gt["supd"])
            em.wait("tensor", sDV, gt["normread"])
            em.wait("tensor", sDV, gt["sev"])
            gt["silu"] = []
            gt["mul"] = []
            gt["down"] = {}
            for q6 in range(6):
                for hcl in range(4):
                    hc = 4 * q6 + hcl
                    pg = (P0, P1)[hc % 2]
                    pu = (P4, P7)[hc % 2]
                    if hc >= 2:
                        em.wait("tensor", sAC, gt["silu"][hc - 2])
                        em.wait("tensor", sDV, gt["mul"][hc - 2])
                    for kc in range(8):
                        em.op("tensor", "matmul", pg[:, 0:256],
                              WgT[:, HID * kc + 128 * hc:HID * kc + 128 * (hc + 1)],
                              xnfM[:, 256 * kc:256 * (kc + 1)],
                              start=(kc == 0), stop=(kc == 7))
                    for kc in range(8):
                        kw = dict(inc=sPE) if kc == 7 else {}
                        r = em.op("tensor", "matmul", pu[:, 0:256],
                                  WuT[:, HID * kc + 128 * hc:HID * kc + 128 * (hc + 1)],
                                  xnfM[:, 256 * kc:256 * (kc + 1)],
                                  start=(kc == 0), stop=(kc == 7), **kw)
                    em.wait("scalar", sPE, r)
                    if hc >= 1:
                        em.wait("scalar", sDV, gt["mul"][hc - 1])
                    sl = em.op("scalar", "activation", silu_t[hc % 2][:, :],
                               pg[:, 0:256], AF.Silu, scale=SC1, inc=sAC)
                    gt["silu"].append(sl)
                    em.wait("vector", sAC, sl)
                    if q6 >= 1:
                        em.wait("vector", sPE, gt["down"][(q6 - 1, hcl)])
                    ml = em.op("vector", "tensor_tensor",
                               hp[:, 256 * hcl:256 * (hcl + 1)],
                               silu_t[hc % 2][:, :], pu[:, 0:256], op=ALU.mult,
                               inc=sDV)
                    gt["mul"].append(ml)
                if b_sub < 4:
                    for hcl in range(4):
                        gt["down"][(q6, hcl)] = 0
                    continue
                em.wait("tensor", sAC, M["wdt"])
                for hcl in range(4):
                    hc = 4 * q6 + hcl
                    em.wait("tensor", sDV, gt["mul"][hc])
                    for tt in range(2):
                        pd = (P23, P56)[tt]
                        for ncol in range(2):
                            kw = dict(inc=sPE) if (ncol == 1 and tt == 1) else {}
                            r = em.op("tensor", "matmul",
                                      pd[:, 512 * ncol:512 * (ncol + 1)],
                                      hp[:, 256 * hcl + 128 * tt:
                                         256 * hcl + 128 * (tt + 1)],
                                      WdT[:, E * hc + 512 * ncol:
                                          E * hc + 512 * (ncol + 1)],
                                      start=(hc == 0), stop=(hc == 23), **kw)
                    gt["down"][(q6, hcl)] = r
            gt["ffn_gu"] = sPE.n
            gt["ffn_mul_last"] = gt["mul"][-1]
            gt["ffn_silu_last"] = gt["silu"][-1]
            gt["down_done"] = sPE.n
            if b_sub < 4:
                continue
            for tt, td in ((0, t0), (1, t1)):
                pd = (P23, P56)[tt]
                em.wait("scalar", sPE, gt["down_done"])
                prev_ov = gt.get("ov_last") if tt == 1 else                     (g[t0 - 1].get("ov_last") if t0 >= 1 else None)
                if prev_ov:
                    em.wait("scalar", sDV, prev_ov)
                gk = em.op("scalar", "activation", g_bf[:, :], pd.ap(), AF.Silu,
                           scale=SC2, inc=sAC)
                g[td]["g_mark"] = gk
                if debug:
                    em.wait("gpsimd", sAC, gk)
                    dv_ = em.dma("gpsimd", dbg_g[td], g_bf[:, :], sGP)
                    em.wait("scalar", sGP, dv_)
                em.wait("vector", sAC, gk)
                for hf in range(2):
                    ob = out_sb[hf]
                    if td >= 1 or tt == 1:
                        em.wait("vector", sGP, g[td - 1]["st_y"] if tt == 0 else 0)
                    prev = gt.get("last_st_y_" + str(hf))
                    if prev:
                        em.wait("vector", sGP, prev)
                    em.wait("vector", sDV, gt["sbf_sup"])
                    ov = em.op("vector", "tensor_tensor", ob[:, :],
                               s_sup[:, 1024 * tt + 512 * hf:1024 * tt + 512 * (hf + 1)],
                               g_bf[:, 512 * hf:512 * (hf + 1)],
                               op=ALU.mult, inc=sDV)
                    gt["ov_last"] = ov
                    g[td]["ov_last"] = ov
                    em.wait("gpsimd", sDV, ov)
                    av = em.dma("gpsimd", ob[:, :], x3[td][:, 512 * hf:512 * (hf + 1)],
                                sGP, accum_op=ALU.add)
                    em.wait("gpsimd", sGP, av)
                    stv = em.dma("gpsimd", y3[td][:, 512 * hf:512 * (hf + 1)], ob[:, :], sGP)
                    gt["last_st_y_" + str(hf)] = stv
                g[td]["st_y"] = stv

    em.wait("gpsimd", sGP, sGP.n)
    return _finish(nc, em)


def _finish(nc, em):
    with nc.Block() as block:
        @block.sync
        def _(eng):
            for f in em.ops["sync"]:
                f(eng)

        @block.gpsimd
        def _(eng):
            for f in em.ops["gpsimd"]:
                f(eng)

        @block.scalar
        def _(eng):
            for f in em.ops["scalar"]:
                f(eng)

        @block.vector
        def _(eng):
            for f in em.ops["vector"]:
                f(eng)

        @block.tensor
        def _(eng):
            for f in em.ops["tensor"]:
                f(eng)

    return nc


_NC_CACHE = {}


def _get_nc(debug=False):
    if debug not in _NC_CACHE:
        _NC_CACHE[debug] = build(debug)
    return _NC_CACHE[debug]


def make_inmaps(L0_attn_x, L0_attn_qkv_w, L0_attn_qkv_b, L0_attn_out_w, L0_attn_out_b,
                L0_ffn_w_gate, L0_ffn_w_up, L0_ffn_w_down):
    bf = ml_dtypes.bfloat16
    ident = np.eye(128, dtype=bf)
    mask4 = np.tile(np.triu(np.ones((128, 128), dtype=bf)), (1, 4))
    ones1 = np.ones((128, 1), dtype=bf)
    ones2 = np.ones((2, 128), dtype=bf)
    wqkv = np.ascontiguousarray(np.asarray(L0_attn_qkv_w, dtype=np.float32))
    bqkv = np.ascontiguousarray(
        np.asarray(L0_attn_qkv_b, dtype=np.float32).reshape(3, E))
    wout = np.ascontiguousarray(np.asarray(L0_attn_out_w, dtype=np.float32))
    bout = np.ascontiguousarray(
        np.asarray(L0_attn_out_b, dtype=np.float32).reshape(1, E))
    wg = np.ascontiguousarray(np.asarray(L0_ffn_w_gate, dtype=np.int32))
    wu = np.ascontiguousarray(np.asarray(L0_ffn_w_up, dtype=np.int32))
    wd = np.ascontiguousarray(np.asarray(L0_ffn_w_down, dtype=np.int32))
    x = np.asarray(L0_attn_x, dtype=np.float32)
    in_maps = []
    for core in range(NCORE):
        b = core // 4
        s0 = (core % 4) * TOK
        mv = np.zeros((64, 4), np.float32)
        mv[:, :core % 4] = 1.0
        in_maps.append({
            "x": np.ascontiguousarray(x[b, s0:s0 + TOK, :]),
            "wqkv": wqkv, "bqkv": bqkv, "wout": wout, "bout": bout,
            "wg": wg, "wu": wu, "wd": wd,
            "ident": ident, "mask4": mask4, "ones1": ones1, "ones2": ones2,
            "maskv": mv,
        })
    return in_maps


def kernel(L0_attn_x, L0_attn_qkv_w, L0_attn_qkv_b, L0_attn_out_w, L0_attn_out_b,
           L0_ffn_w_gate, L0_ffn_w_up, L0_ffn_w_down):
    nc = _get_nc(False)
    in_maps = make_inmaps(L0_attn_x, L0_attn_qkv_w, L0_attn_qkv_b, L0_attn_out_w,
                          L0_attn_out_b, L0_ffn_w_gate, L0_ffn_w_up, L0_ffn_w_down)
    res = run_bass_kernel_spmd(nc, in_maps, core_ids=list(range(NCORE)))
    out = np.empty((B, L, E), dtype=np.float32)
    for core in range(NCORE):
        b = core // 4
        s0 = (core % 4) * TOK
        out[b, s0:s0 + TOK, :] = np.asarray(res.results[core]["y"])
    return out


# version 07
# speedup vs baseline: 1.0717x; 1.0717x over previous
import numpy as np
import ml_dtypes
import concourse.bass as bass
import concourse.mybir as mybir
from concourse.bass_utils import run_bass_kernel_spmd

F32 = mybir.dt.float32
BF16 = mybir.dt.bfloat16
I32 = mybir.dt.int32
U8 = mybir.dt.uint8
AF = mybir.ActivationFunctionType
ALU = mybir.AluOpType
AX = mybir.AxisListType

B, L, E, NH, DH = 2, 8192, 1024, 16, 64
TOK = 2048
NT = TOK // 128
HID = 3072
NCORE = 8
EPS_LN = 1e-5
EPS_AT = 1e-6
SC1 = 1.0 / 127.0
SC2 = 1.0 / (127.0 * 127.0)


class Ctr:
    def __init__(self, stack, nc, name):
        self.h = stack.enter_context(nc.semaphore(name))
        self.n = 0


class Em:
    ENGINES = ("sync", "gpsimd", "scalar", "vector", "tensor")

    def __init__(self):
        self.ops = {k: [] for k in self.ENGINES}
        self.lastwait = {k: {} for k in self.ENGINES}

    def wait(self, eng, ctr, val=None):
        v = ctr.n if val is None else val
        if v is None or v <= 0:
            return
        key = id(ctr)
        if self.lastwait[eng].get(key, -1) >= v:
            return
        self.lastwait[eng][key] = v
        h = ctr.h
        self.ops[eng].append(lambda e: e.wait_ge(h, v))

    def dma(self, eng, out, in_, ctr, **kw):
        h = ctr.h
        self.ops[eng].append(lambda e: e.dma_start(out=out, in_=in_, **kw).then_inc(h, 16))
        ctr.n += 16
        return ctr.n

    def op(self, eng, meth, *args, inc=None, **kw):
        if inc is not None:
            h = inc.h
            inc.n += 1
            v = inc.n
            self.ops[eng].append(
                lambda e: getattr(e, meth)(*args, **kw).then_inc(h, 1))
            return v
        self.ops[eng].append(lambda e: getattr(e, meth)(*args, **kw))
        return None

    def raw(self, eng, fn):
        self.ops[eng].append(fn)


def build(debug=False, n_t=NT, phase_limit=3, b_sub=4):
    nc = bass.Bass("TRN2", target_bir_lowering=False, detect_race_conditions=False)
    okind = "ExternalOutput" if debug else "Internal"

    x_d = nc.dram_tensor("x", [n_t * 128, E], F32, kind="ExternalInput").ap()
    wqkv_d = nc.dram_tensor("wqkv", [E, 3 * E], F32, kind="ExternalInput").ap()
    bqkv_d = nc.dram_tensor("bqkv", [3, E], F32, kind="ExternalInput").ap()
    wout_d = nc.dram_tensor("wout", [E, E], F32, kind="ExternalInput").ap()
    bout_d = nc.dram_tensor("bout", [1, E], F32, kind="ExternalInput").ap()
    wg_d = nc.dram_tensor("wg", [HID, E], I32, kind="ExternalInput").ap()
    wu_d = nc.dram_tensor("wu", [HID, E], I32, kind="ExternalInput").ap()
    wd_d = nc.dram_tensor("wd", [E, HID], I32, kind="ExternalInput").ap()
    ident_d = nc.dram_tensor("ident", [128, 128], BF16, kind="ExternalInput").ap()
    mask4_d = nc.dram_tensor("mask4", [128, 512], BF16, kind="ExternalInput").ap()
    ones1_d = nc.dram_tensor("ones1", [128, 1], BF16, kind="ExternalInput").ap()
    ones2_d = nc.dram_tensor("ones2", [2, 128], BF16, kind="ExternalInput").ap()
    maskv_d = nc.dram_tensor("maskv", [64, 4], F32, kind="ExternalInput").ap()

    y_d = nc.dram_tensor("y", [n_t * 128, E], F32, kind="ExternalOutput").ap()

    wgbf_d = nc.dram_tensor("wgbf", [HID, E], BF16, kind="Internal").ap()
    wubf_d = nc.dram_tensor("wubf", [HID, E], BF16, kind="Internal").ap()
    wdbf_d = nc.dram_tensor("wdbf", [E, HID], BF16, kind="Internal").ap()
    b2q_d = nc.dram_tensor("b2q", [2, 3 * E], BF16, kind="Internal").ap()
    b2o_d = nc.dram_tensor("b2o", [2, E], BF16, kind="Internal").ap()
    xnf_d = nc.dram_tensor("xnf", [n_t // 2, 8, 128, 256], BF16, kind=okind).ap()
    qk_d = nc.dram_tensor("qfm", [n_t, 128, 2 * E], BF16, kind=okind).ap()
    ktv_d = nc.dram_tensor("ktv", [n_t, 128, 2 * E], BF16, kind=okind).ap()
    ccin_d = nc.dram_tensor("ccin", [64, 1040], F32, kind="Internal").ap()
    ccout_d = nc.dram_tensor("ccout", [256, 1040], F32, kind="Internal").ap()
    if debug:
        dbg_sloc = nc.dram_tensor("dbg_sloc", [64, 1040], F32, kind="ExternalOutput").ap()
        dbg_s0 = nc.dram_tensor("dbg_s0", [64, 1040], F32, kind="ExternalOutput").ap()
        dbg_s = nc.dram_tensor("dbg_s", [n_t, 128, E], BF16, kind="ExternalOutput").ap()
        dbg_g = nc.dram_tensor("dbg_g", [n_t, 128, E], BF16, kind="ExternalOutput").ap()
        dbg_spre = nc.dram_tensor("dbg_spre", [n_t, 128, E], BF16, kind="ExternalOutput").ap()
        dbg_nrm = nc.dram_tensor("dbg_nrm", [n_t, 128, 32], F32, kind="ExternalOutput").ap()

    WgT = nc.alloc_sbuf_tensor("WgT", [128, 8 * HID], BF16)
    WuT = nc.alloc_sbuf_tensor("WuT", [128, 8 * HID], BF16)
    R3 = nc.alloc_sbuf_tensor("R3", [128, 8 * HID], BF16)
    Wqkv = R3
    WdT = R3
    Wout = nc.alloc_sbuf_tensor("Wout", [128, 8 * E], BF16)
    identS = nc.alloc_sbuf_tensor("identS", [128, 128], BF16)
    mask4S = nc.alloc_sbuf_tensor("mask4S", [128, 512], BF16)
    ones1S = nc.alloc_sbuf_tensor("ones1S", [128, 1], BF16)
    ones2S = nc.alloc_sbuf_tensor("ones2S", [2, 128], BF16)
    maskvS = nc.alloc_sbuf_tensor("maskvS", [64, 4], F32)

    UN_BYTES = 47232
    un = nc.alloc_sbuf_tensor("UN", [128, UN_BYTES], U8)
    ub = nc.lookup_mloc(un).addr

    def at(name, shape, dtype, off):
        return nc.alloc_sbuf_tensor_at(name, shape, dtype, offset=ub + off)

    xt = [at("xt0", [128, E], F32, 0), at("xt1", [128, E], F32, 4096)]
    em_t = at("em", [128, 512], BF16, 8192)
    qbf = at("qbf", [128, E], BF16, 9216)
    ktvA = at("ktvA", [128, 2 * E], BF16, 11264)
    qfmA = at("qfmA", [128, E], BF16, 15360)
    kfmA = at("kfmA", [128, E], BF16, 17408)
    workA = at("workA", [128, E], F32, 19456)
    etmp = at("etmp", [128, 512], F32, 19456)
    xnbf = at("xnbf", [128, E], BF16, 23552)
    xnfmA = at("xnfmA", [128, E], BF16, 25600)
    sloc = [at("sloc0", [64, 1040], F32, 27648), at("sloc1", [64, 1040], F32, 31872)]
    bias2q = at("bias2q", [2, 3 * E], BF16, 36096)
    sums = [at(f"sums{i}", [128, 1], F32, 42240 + 384 * i) for i in range(2)]
    negm = [at(f"negm{i}", [128, 1], F32, 42304 + 384 * i) for i in range(2)]
    ssq = [at(f"ssq{i}", [128, 1], F32, 42368 + 384 * i) for i in range(2)]
    lnv = [at(f"lnv{i}", [128, 1], F32, 42432 + 384 * i) for i in range(2)]
    rstd = [at(f"rstd{i}", [128, 1], F32, 42496 + 384 * i) for i in range(2)]
    mbias = [at(f"mbias{i}", [128, 1], F32, 42560 + 384 * i) for i in range(2)]
    stg = [at("stg0", [128, E], BF16, 43008), at("stg1", [128, E], BF16, 45056)]
    bstq = at("bstq", [3, E], F32, 0)
    bsto = at("bsto", [1, E], F32, 4096)
    bhiq = at("bhiq", [3, E], BF16, 8192)
    bloq = at("bloq", [3, E], BF16, 11264)
    bhio = at("bhio", [1, E], BF16, 15360)
    bloo = at("bloo", [1, E], BF16, 17408)

    S_run = [at("Srun0", [64, 1024], F32, 0), at("Srun1", [64, 1024], F32, 4096)]
    S_bf = at("Sbf", [64, 1024], BF16, 8192)
    z_run = [at("zrun0", [64, 16], F32, 10240), at("zrun1", [64, 16], F32, 10304)]
    z_bf = at("zbf", [64, 16], BF16, 10368)
    norm_eps = at("normeps", [128, 16], F32, 10432)
    recip = at("recip", [128, 16], F32, 10496)
    qkM = at("qkM", [64, 4 * E], BF16, 10624)
    ktvM = at("ktvM", [128, 2 * E], BF16, 18816)
    at_bf2 = [at("atbf0", [128, 512], BF16, 22912), at("atbf1", [128, 512], BF16, 37248)]
    s_pre = at("spre", [128, E], BF16, 23936)
    sfm = at("sfm", [128, E], BF16, 23936)
    s_sup = at("ssup", [128, 2 * E], BF16, 25984)
    xnfM = at("xnfM", [128, 2048], BF16, 30080)
    hp = at("hp", [128, 1024], BF16, 34176)
    silu_t = [at("silu0", [128, 256], F32, 36224), at("silu1", [128, 256], F32, 36224)]
    g_bf = at("gbf", [128, E], BF16, 38272)
    out_sb = [at("outsb0", [128, 512], F32, 40320), at("outsb1", [128, 512], F32, 42368)]
    bias2o = at("bias2o", [2, E], BF16, 44416)
    gb = [at("gb0", [64, 1040], F32, 23936), at("gb1", [64, 1040], F32, 32256)]
    gterm = at("gterm", [64, 1040], F32, 38272)
    gacc = [at("gacc0", [64, 1040], F32, 42432), at("gacc1", [64, 1040], F32, 28096)]

    P0 = nc.alloc_psum_tensor("P0", [128, 512], F32)
    P1 = nc.alloc_psum_tensor("P1", [128, 512], F32)
    P23 = nc.alloc_psum_tensor("P23", [128, 1024], F32)
    P4 = nc.alloc_psum_tensor("P4", [128, 512], F32)
    P56 = nc.alloc_psum_tensor("P56", [128, 1024], F32)
    P7 = nc.alloc_psum_tensor("P7", [128, 512], F32)
    P0b = P0.ap().bitcast(BF16)
    P1b = P1.ap().bitcast(BF16)
    P23b = P23.ap().bitcast(BF16)

    from contextlib import ExitStack
    stack = ExitStack()
    sSY = Ctr(stack, nc, "sSY")
    sGP = Ctr(stack, nc, "sGP")
    sPE = Ctr(stack, nc, "sPE")
    sDV = Ctr(stack, nc, "sDV")
    sAC = Ctr(stack, nc, "sAC")
    sCC = Ctr(stack, nc, "sCC")

    em = Em()
    M = {}

    em.dma("sync", identS[:, :], ident_d, sSY)
    em.dma("sync", mask4S[:, :], mask4_d, sSY)
    em.dma("sync", ones1S[:, :], ones1_d, sSY)
    em.dma("sync", ones2S[:, :], ones2_d, sSY)
    em.dma("sync", maskvS[:, :], maskv_d, sSY)
    em.dma("sync", bstq[:, :], bqkv_d, sSY)
    em.dma("sync", bsto[:, :], bout_d, sSY)
    M["consts"] = sSY.n

    wq3 = wqkv_d.rearrange("(c p) n -> c p n", p=128)
    for c in range(8):
        em.dma("gpsimd", Wqkv[:, c * 3 * E:(c + 1) * 3 * E], wq3[c], sGP)
    M["wqkv"] = sGP.n
    wo3 = wout_d.rearrange("(c p) n -> c p n", p=128)
    for c in range(8):
        em.dma("gpsimd", Wout[:, c * E:(c + 1) * E], wo3[c], sGP)
    M["wout"] = sGP.n

    em.wait("vector", sSY, M["consts"])
    _bq = em.op("vector", "tensor_copy", bhiq[:, :], bstq[:, :], inc=sDV)
    em.wait("vector", sDV, _bq)
    em.op("vector", "tensor_tensor", bloq[:, :], bstq[:, :], bhiq[:, :], op=ALU.subtract)
    _bo = em.op("vector", "tensor_copy", bhio[:, :], bsto[:, :], inc=sDV)
    em.wait("vector", sDV, _bo)
    M["biasdv"] = em.op("vector", "tensor_tensor", bloo[:, :], bsto[:, :], bhio[:, :],
                        op=ALU.subtract, inc=sDV)
    em.wait("gpsimd", sDV, M["biasdv"])
    em.dma("gpsimd", b2q_d[0:1, :].rearrange("a (b c) -> (a b) c", b=3), bhiq[:, :], sGP)
    em.dma("gpsimd", b2q_d[1:2, :].rearrange("a (b c) -> (a b) c", b=3), bloq[:, :], sGP)
    em.dma("gpsimd", b2o_d[0:1, :], bhio[:, :], sGP)
    em.dma("gpsimd", b2o_d[1:2, :], bloo[:, :], sGP)
    M["biasbounce"] = sGP.n
    em.wait("sync", sGP, M["biasbounce"])
    em.dma("sync", bias2q[:, :], b2q_d, sSY)
    M["bias2"] = sSY.n

    def load_act_table(set_name):
        def f(e):
            from concourse.hw_specs import get_activation_tables
            names = list(get_activation_tables(nc.m.arch).keys())
            e.add_instruction(mybir.InstLoadActFuncSet(
                name=nc.get_next_instruction_name(),
                act_func_set_id=names.index(set_name), ins=[], outs=[]))
        em.raw("scalar", f)

    load_act_table("natural_log_exp_and_others")

    bounce_jobs = []
    for src, dst in ((wg_d, wgbf_d), (wu_d, wubf_d), (wd_d, wdbf_d)):
        s4 = src.rearrange("(t p) (c d) -> t c p d", p=128, d=1024)
        d4 = dst.rearrange("(t p) (c d) -> t c p d", p=128, d=1024)
        for t in range(s4.shape[0]):
            for c in range(s4.shape[1]):
                bounce_jobs.append((s4[t, c], d4[t, c]))
    bstate = {"i": 0, "hist": []}

    def emit_bounce(n):
        for _ in range(n):
            i = bstate["i"]
            if i >= len(bounce_jobs):
                return
            src, dst = bounce_jobs[i]
            b = stg[i % 2]
            if i >= 2:
                em.wait("gpsimd", sGP, bstate["hist"][i - 2][1])
            ldv = em.dma("gpsimd", b[:, :], src, sGP)
            em.wait("gpsimd", sGP, ldv)
            stv = em.dma("gpsimd", dst, b[:, :], sGP)
            bstate["hist"].append((ldv, stv))
            bstate["i"] += 1

    x3 = x_d.rearrange("(t p) d -> t p d", p=128)
    y3 = y_d.rearrange("(t p) d -> t p d", p=128)

    if phase_limit < 1:
        em.wait("gpsimd", sGP, sGP.n)
        em.wait("sync", sSY, sSY.n)
        return _finish(nc, em)

    h = [dict() for _ in range(n_t)]

    def emit_xload(t):
        ht = h[t]
        if t == 0:
            em.wait("sync", sDV, M["biasdv"])
        if t >= 2:
            em.wait("sync", sAC, h[t - 2]["xnbf"])
        ht["x"] = em.dma("sync", xt[t % 2][:, :], x3[t], sSY)

    def emit_ln(t):
        ht = h[t]
        p2 = t % 2
        em.wait("vector", sSY, ht["x"])
        if t == 0:
            em.wait("vector", sGP, M["biasbounce"])
        _rv = em.op("vector", "reduce_sum", sums[p2][:, :], xt[t % 2][:, :],
                    axis=AX.X, inc=sDV)
        em.wait("vector", sDV, _rv)
        ht["negm"] = em.op("vector", "tensor_scalar", negm[p2][:, :], sums[p2][:, :],
                           -1.0 / E, None, op0=ALU.mult, inc=sDV)
        em.wait("scalar", sSY, ht["x"])
        em.wait("scalar", sDV, ht["negm"])
        if t >= 1:
            em.wait("scalar", sPE, h[t - 1]["tr_xn"])
        sqv = em.op("scalar", "activation", xnbf[:, :], xt[t % 2][:, :], AF.Square,
                    bias=negm[p2][:, :], accum_out=ssq[p2][:, :], inc=sAC)
        em.wait("vector", sAC, sqv)
        veps = em.op("vector", "tensor_scalar", lnv[p2][:, :], ssq[p2][:, :], 1.0 / E,
                     EPS_LN, op0=ALU.mult, op1=ALU.add, inc=sDV)
        em.wait("scalar", sDV, veps)
        em.op("scalar", "activation", lnv[p2][:, :], lnv[p2][:, :], AF.Ln)
        ht["rstd"] = em.op("scalar", "activation", rstd[p2][:, :], lnv[p2][:, :],
                           AF.Exp, scale=-0.5, inc=sAC)
        em.wait("vector", sAC, ht["rstd"])
        em.wait("vector", sDV, ht["negm"])
        ht["mb"] = em.op("vector", "tensor_scalar", mbias[p2][:, :], negm[p2][:, :],
                         rstd[p2][:, :], None, op0=ALU.mult, inc=sDV)
        em.wait("scalar", sDV, ht["mb"])
        ht["xnbf"] = em.op("scalar", "activation", xnbf[:, :], xt[t % 2][:, :],
                           AF.Identity, bias=mbias[p2][:, :], scale=rstd[p2][:, :],
                           inc=sAC)

    def emit_trxn(t):
        ht = h[t]
        em.wait("tensor", sSY, M["consts"])
        em.wait("tensor", sAC, ht["xnbf"])
        if t >= 1:
            em.wait("tensor", sDV, h[t - 1]["kfm_ev"])
        for j in range(8):
            kw = dict(inc=sPE) if j == 7 else {}
            r = em.op("tensor", "transpose", P23b[:, 128 * j:128 * (j + 1)],
                      xnbf[:, 128 * j:128 * (j + 1)], identS[:, :], **kw)
        ht["tr_xn"] = r
        em.wait("vector", sPE, ht["tr_xn"])
        if t >= 1:
            em.wait("vector", sGP, h[t - 1]["st_xnf"])
            em.wait("vector", sPE, h[t - 1]["qkv_done"])
        ht["xnfm_ev"] = em.op("vector", "tensor_copy", xnfmA[:, :], P23b[:, 0:1024],
                              inc=sDV)
        em.wait("gpsimd", sDV, ht["xnfm_ev"])
        dst = xnf_d[t // 2][:, :, (t % 2) * 128:(t % 2) * 128 + 128].rearrange(
            "k p u -> p k u")
        ht["st_xnf"] = em.dma("gpsimd", dst,
                              xnfmA[:, :].rearrange("p (k u) -> p k u", k=8), sGP)

    def emit_qkv(t):
        ht = h[t]
        em.wait("tensor", sGP, M["wqkv"])
        em.wait("tensor", sSY, M["bias2"])
        em.wait("tensor", sDV, ht["xnfm_ev"])
        ht["elu"] = [None] * 6
        ht["qkv_cc"] = []
        for cc in range(6):
            pb = (P0, P1, P7)[cc % 3]
            if cc >= 3:
                em.wait("tensor", sDV, ht["elu"][cc - 3])
            elif t >= 1:
                em.wait("tensor", sDV, h[t - 1]["elu"][cc + 3])
            for kc in range(8):
                em.op("tensor", "matmul", pb[:, :],
                      xnfmA[:, 128 * kc:128 * (kc + 1)],
                      Wqkv[:, 3 * E * kc + 512 * cc:3 * E * kc + 512 * (cc + 1)],
                      start=(kc == 0), stop=False)
            v = em.op("tensor", "matmul", pb[:, :], ones2S[:, :],
                      bias2q[:, 512 * cc:512 * (cc + 1)], start=False, stop=True,
                      inc=sPE)
            ht["qkv_cc"].append(v)
            if cc < 4:
                em.wait("scalar", sPE, v)
                if cc >= 1:
                    em.wait("scalar", sDV, ht["elu"][cc - 1])
                elif t >= 1:
                    em.wait("scalar", sDV, h[t - 1]["elu"][3])
                ev = em.op("scalar", "activation", etmp[:, :], pb[:, :], AF.Exp,
                           inc=sAC)
                em.wait("vector", sAC, ev)
                _mv = em.op("vector", "tensor_scalar", em_t[:, :], etmp[:, :], 1.0,
                            None, op0=ALU.min, inc=sDV)
                em.wait("vector", sDV, _mv)
                if cc < 2:
                    dest = qbf[:, 512 * cc:512 * (cc + 1)]
                    if t >= 1:
                        em.wait("vector", sPE, h[t - 1]["tr_q"])
                else:
                    dest = ktvA[:, 512 * (cc - 2):512 * (cc - 1)]
                    if t >= 1:
                        em.wait("vector", sPE, h[t - 1]["ds"])
                        em.wait("vector", sSY, h[t - 1]["st_ktv"])
                ht["elu"][cc] = em.op("vector", "scalar_tensor_tensor", dest,
                                      pb[:, :], 0.0, em_t[:, :],
                                      op0=ALU.max, op1=ALU.add, inc=sDV)
            else:
                em.wait("vector", sPE, v)
                if t >= 1:
                    em.wait("vector", sPE, h[t - 1]["ds"])
                    em.wait("vector", sSY, h[t - 1]["st_ktv"])
                dest = ktvA[:, 1024 + 512 * (cc - 4):1024 + 512 * (cc - 3)]
                ht["elu"][cc] = em.op("vector", "tensor_copy", dest, pb[:, :], inc=sDV)
        ht["qkv_done"] = ht["qkv_cc"][5]

    def emit_trqk(t):
        ht = h[t]
        em.wait("tensor", sDV, ht["elu"][1])
        if t >= 1:
            em.wait("tensor", sDV, h[t - 1]["qfm_ev"])
        for j in range(8):
            kw = dict(inc=sPE) if j == 7 else {}
            r = em.op("tensor", "transpose",
                      P23b[:, 1024 + 128 * j:1024 + 128 * (j + 1)],
                      qbf[:, 128 * j:128 * (j + 1)], identS[:, :], **kw)
        ht["tr_q"] = r
        em.wait("tensor", sDV, ht["elu"][3])
        em.wait("tensor", sDV, ht["xnfm_ev"])
        for j in range(8):
            kw = dict(inc=sPE) if j == 7 else {}
            r = em.op("tensor", "transpose", P23b[:, 128 * j:128 * (j + 1)],
                      ktvA[:, 128 * j:128 * (j + 1)], identS[:, :], **kw)
        ht["tr_k"] = r
        em.wait("vector", sPE, ht["tr_q"])
        if t >= 1:
            em.wait("vector", sGP, h[t - 1]["st_qfm"])
        ht["qfm_ev"] = em.op("vector", "tensor_copy", qfmA[:, :], P23b[:, 1024:2048],
                             inc=sDV)
        em.wait("vector", sPE, ht["tr_k"])
        if t >= 1:
            em.wait("vector", sGP, h[t - 1]["st_kfm"])
        ht["kfm_ev"] = em.op("vector", "tensor_copy", kfmA[:, :], P23b[:, 0:1024],
                             inc=sDV)
        em.wait("gpsimd", sDV, ht["qfm_ev"])
        em.dma("gpsimd", qk_d[t][:, 0:1024], qfmA[:, :], sGP)
        em.wait("gpsimd", sDV, ht["kfm_ev"])
        ht["st_qfm"] = em.dma("gpsimd", qk_d[t][:, 1024:2048], kfmA[:, :], sGP)
        ht["st_kfm"] = ht["st_qfm"]
        em.wait("sync", sDV, ht["elu"][5])
        ht["st_ktv"] = em.dma("sync", ktv_d[t], ktvA[:, :], sSY)

    def emit_ds(t):
        ht = h[t]
        em.wait("tensor", sDV, ht["elu"][3])
        em.wait("tensor", sDV, ht["elu"][5])
        if t >= 1:
            em.wait("tensor", sDV, h[t - 1]["sloc"])
        for hh in range(NH):
            em.op("tensor", "matmul", P56[0:64, 64 * hh:64 * (hh + 1)],
                  ktvA[:, 64 * hh:64 * (hh + 1)],
                  ktvA[:, 1024 + 64 * hh:1024 + 64 * (hh + 1)], start=True, stop=True)
            kw = dict(inc=sPE) if hh == NH - 1 else {}
            r = em.op("tensor", "matmul", P4[0:64, 16 + hh:17 + hh],
                      ktvA[:, 64 * hh:64 * (hh + 1)], ones1S[:, :],
                      start=True, stop=True, **kw)
        ht["ds"] = r

    def emit_sloc(t):
        ht = h[t]
        em.wait("vector", sPE, ht["ds"])
        cur, prv = sloc[t % 2], sloc[(t + 1) % 2]
        if t == 0:
            em.op("vector", "tensor_copy", cur[:, 0:1024], P56[0:64, :], inc=sDV)
            r = em.op("vector", "tensor_copy", cur[:, 1024:1040], P4[0:64, 16:32],
                      inc=sDV)
        else:
            em.wait("vector", sDV, h[t - 1]["sloc"])
            em.op("vector", "tensor_add", cur[:, 0:1024], P56[0:64, :],
                  prv[:, 0:1024], inc=sDV)
            r = em.op("vector", "tensor_add", cur[:, 1024:1040], P4[0:64, 16:32],
                      prv[:, 1024:1040], inc=sDV)
        ht["sloc"] = r

    emit_xload(0)
    emit_ln(0)
    for t in range(n_t):
        if t + 1 < n_t:
            emit_xload(t + 1)
        emit_trxn(t)
        if t >= 1:
            emit_ds(t - 1)
            emit_sloc(t - 1)
        if t + 1 < n_t:
            emit_ln(t + 1)
        emit_qkv(t)
        emit_trqk(t)
        emit_bounce(5)
    emit_ds(n_t - 1)
    emit_sloc(n_t - 1)

    emit_bounce(len(bounce_jobs))
    M["bounce_wg"] = bstate["hist"][23][1]
    M["bounce_wu"] = bstate["hist"][47][1]
    M["bounce_wd"] = bstate["hist"][71][1]

    if phase_limit < 2:
        em.wait("gpsimd", sGP, sGP.n)
        return _finish(nc, em)
    em.wait("gpsimd", sDV, h[n_t - 1]["sloc"])
    slocF = sloc[(n_t - 1) % 2]
    ccin_st = em.dma("gpsimd", ccin_d, slocF[:, :], sGP)
    em.wait("gpsimd", sGP, ccin_st)
    ccv = em.op("gpsimd", "collective_compute", "AllGather", ALU.bypass,
                replica_groups=[[0, 1, 2, 3], [4, 5, 6, 7]],
                ins=[ccin_d.opt()], outs=[ccout_d.opt()], inc=sCC)
    if debug:
        em.dma("gpsimd", dbg_sloc, ccin_d, sGP)

    em.wait("scalar", sGP, M["bounce_wg"])
    for dc in range(8):
        em.dma("scalar", WgT[:, HID * dc:HID * (dc + 1)],
               wgbf_d[:, 128 * dc:128 * (dc + 1)], sAC, transpose=True)
    em.wait("scalar", sGP, M["bounce_wu"])
    for dc in range(8):
        em.dma("scalar", WuT[:, HID * dc:HID * (dc + 1)],
               wubf_d[:, 128 * dc:128 * (dc + 1)], sAC, transpose=True)
    M["wgut"] = sAC.n
    em.wait("scalar", sPE, h[n_t - 1]["qkv_done"])
    em.wait("scalar", sGP, M["bounce_wd"])
    for hc in range(24):
        em.dma("scalar", WdT[:, E * hc:E * (hc + 1)],
               wdbf_d[:, 128 * hc:128 * (hc + 1)], sAC, transpose=True)
    M["wdt"] = sAC.n
    cco = ccout_d.rearrange("(r p) f -> r p f", p=64)
    prevmul = None
    for r in range(4):
        em.wait("sync", sCC, ccv)
        if r >= 2:
            em.wait("sync", sDV, M[f"gmul{r - 2}"])
        gl = em.dma("sync", gb[r % 2][:, :], cco[r], sSY)
        em.wait("vector", sSY, gl)
        if r == 0:
            M["gmul0"] = em.op("vector", "tensor_scalar", gacc[0][:, :], gb[0][:, :],
                               maskvS[:, 0:1], None, op0=ALU.mult, inc=sDV)
            M["gacc0"] = M["gmul0"]
        else:
            M[f"gmul{r}"] = em.op("vector", "tensor_scalar", gterm[:, :],
                                  gb[r % 2][:, :], maskvS[:, r:r + 1], None,
                                  op0=ALU.mult, inc=sDV)
            em.wait("vector", sDV, M[f"gmul{r}"])
            em.wait("vector", sDV, M[f"gacc{r - 1}"])
            M[f"gacc{r}"] = em.op("vector", "tensor_add", gacc[r % 2][:, :],
                                  gterm[:, :], gacc[(r - 1) % 2][:, :], inc=sDV)
    s0 = gacc[1]
    if debug:
        M["s0pre"] = em.op("vector", "engine_nop", inc=sDV)
        em.wait("gpsimd", sDV, M["s0pre"])
        em.dma("gpsimd", dbg_s0, s0[:, :], sGP)
    em.wait("vector", sDV, M["gacc3"])
    _c1 = em.op("vector", "tensor_copy", S_run[0][:, :], s0[:, 0:1024], inc=sDV)
    _c2 = em.op("vector", "tensor_copy", z_run[0][:, :], s0[:, 1024:1040], inc=sDV)
    em.wait("vector", sDV, _c2)
    em.op("vector", "tensor_copy", S_bf[:, :], S_run[0][:, :])
    M["sbf0"] = em.op("vector", "tensor_copy", z_bf[:, :], z_run[0][:, :], inc=sDV)
    M["gapdv"] = M["sbf0"]
    em.wait("sync", sGP, M["bounce_wd"])
    em.wait("sync", sDV, M["gapdv"])
    em.dma("sync", bias2o[:, :], b2o_d, sSY)
    M["bias2o"] = sSY.n
    M["allst"] = sGP.n
    M["allstS"] = sSY.n

    load_act_table("silu_and_others")

    if phase_limit < 3:
        em.wait("gpsimd", sGP, sGP.n)
        em.wait("vector", sDV, sDV.n)
        return _finish(nc, em)

    def head_blk(tens, hh, w):
        return tens[0:64, w * hh:w * (hh + 1)]

    g = [dict() for _ in range(n_t)]

    for t in range(n_t):
        gt = g[t]
        sup = t // 2
        em.wait("sync", sGP, M["allst"])
        em.wait("sync", sSY, M["allstS"])
        em.wait("gpsimd", sGP, M["allst"])
        if t >= 1:
            em.wait("sync", sPE, g[t - 1]["attn_mm"])
            em.wait("gpsimd", sPE, g[t - 1]["attn_mm"])
        qm3 = qkM[0:64, :].rearrange("p (half j u) -> p half j u", half=2, u=128)
        qs3 = qk_d[t].rearrange("p (half u) -> p half u", half=2)
        em.dma("gpsimd", qm3[:, :, 0::2, :], qs3[0:64], sGP)
        gt["ld_q"] = em.dma("gpsimd", qm3[:, :, 1::2, :], qs3[64:128], sGP)
        gt["ld_k"] = gt["ld_q"]
        gt["ld_v"] = em.dma("sync", ktvM[:, :], ktv_d[t], sSY)
        if t % 2 == 0:
            em.wait("sync", sDV, M["gapdv"])
            if sup >= 1:
                em.wait("sync", sPE, g[t - 1]["ffn_gu"])
            gt["ld_xnf"] = em.dma("sync", xnfM[:, :].rearrange("p (k u) -> p k u", k=8),
                                  xnf_d[sup].rearrange("k p u -> p k u"), sSY)

        if b_sub == 11:
            gt["attn_mm"] = 0
            gt["sev"] = 0; gt["sbf_sup"] = 0; gt["normread"] = 0
            gt["supd"] = 0; gt["sbf"] = 0; gt["heads"] = [0, 0, 0, 0]
            continue
        em.wait("tensor", sGP, gt["ld_q"])
        gt["atp"] = []
        gt["mask"] = []
        gt["heads"] = []

        def at_pass(p):
            pb = (P0, P1)[p % 2]
            if p >= 2:
                em.wait("tensor", sDV, gt["mask"][p - 2])
            elif t >= 1:
                em.wait("tensor", sDV, g[t - 1]["sev"])
                if "ffn_silu_last" in g[t - 1]:
                    em.wait("tensor", sAC, g[t - 1]["ffn_silu_last"])
            for hi in range(4):
                hh = 4 * p + hi
                kw = dict(inc=sPE) if hi == 3 else {}
                r = em.op("tensor", "matmul", pb[:, 128 * hi:128 * (hi + 1)],
                          qkM[0:64, 2048 + 128 * hh:2176 + 128 * hh], qkM[0:64, 128 * hh:128 * (hh + 1)],
                          start=True, stop=True, **kw)
            gt["atp"].append(r)

        def mask_pass(p):
            pb = (P0, P1)[p % 2]
            ab = at_bf2[p % 2]
            em.wait("vector", sPE, gt["atp"][p])
            if p >= 2:
                em.wait("vector", sPE, gt["heads"][p - 2])
            elif t >= 1:
                em.wait("vector", sPE, g[t - 1]["heads"][2 + p])
            mk = em.op("vector", "tensor_tensor", ab[:, :], pb[:, :], mask4S[:, :],
                       op=ALU.mult, inc=sDV)
            gt["mask"].append(mk)

        def head_pass(p):
            em.wait("tensor", sDV, gt["mask"][p])
            em.wait("tensor", sDV, M["sbf0"] if t == 0 else g[t - 1]["sbf"])
            em.wait("tensor", sSY, gt["ld_v"])
            if p == 0:
                if t >= 1:
                    em.wait("tensor", sDV, g[t - 1]["sbf_sup"])
                    em.wait("tensor", sDV, g[t - 1]["normread"])
                    em.wait("tensor", sDV, g[t - 1]["supd"])
                if sup >= 1 and t % 2 == 0:
                    em.wait("tensor", sAC, g[t - 2]["g_mark"])
                    em.wait("tensor", sAC, g[t - 1]["g_mark"])
                    em.wait("tensor", sDV, g[t - 1]["ffn_mul_last"])
            ab = at_bf2[p % 2]
            for hi in range(4):
                hh = 4 * p + hi
                em.op("tensor", "matmul", P23[:, 64 * hh:64 * (hh + 1)],
                      ab[:, 128 * hi:128 * (hi + 1)],
                      ktvM[:, 1024 + 64 * hh:1024 + 64 * (hh + 1)],
                      start=True, stop=False)
                em.op("tensor", "matmul", P23[:, 64 * hh:64 * (hh + 1)],
                      qkM[0:64, 128 * hh:128 * (hh + 1)], head_blk(S_bf, hh, 64),
                      start=False, stop=True)
                em.op("tensor", "matmul", P4[:, hh:hh + 1],
                      ab[:, 128 * hi:128 * (hi + 1)], ones1S[:, :],
                      start=True, stop=False)
                em.op("tensor", "matmul", P4[:, hh:hh + 1],
                      qkM[0:64, 128 * hh:128 * (hh + 1)], head_blk(z_bf, hh, 1),
                      start=False, stop=True)
                em.op("tensor", "matmul", P56[0:64, 64 * hh:64 * (hh + 1)],
                      ktvM[:, 64 * hh:64 * (hh + 1)],
                      ktvM[:, 1024 + 64 * hh:1024 + 64 * (hh + 1)],
                      start=True, stop=True)
                kw = dict(inc=sPE) if hi == 3 else {}
                r = em.op("tensor", "matmul", P4[0:64, 16 + hh:17 + hh],
                          ktvM[:, 64 * hh:64 * (hh + 1)], ones1S[:, :],
                          start=True, stop=True, **kw)
            gt["heads"].append(r)

        if b_sub == 12:
            for p in range(4):
                at_pass(p)
                mask_pass(p)
                gt["heads"].append(gt["atp"][p])
            gt["attn_mm"] = gt["atp"][3]
            gt["sev"] = gt["mask"][3]; gt["sbf_sup"] = 0; gt["normread"] = 0
            gt["supd"] = 0; gt["sbf"] = gt["mask"][3]
            continue
        at_pass(0)
        mask_pass(0)
        at_pass(1)
        head_pass(0)
        mask_pass(1)
        at_pass(2)
        head_pass(1)
        mask_pass(2)
        at_pass(3)
        head_pass(2)
        mask_pass(3)
        head_pass(3)
        gt["attn_mm"] = gt["heads"][3]
        if b_sub == 13:
            gt["sev"] = gt["mask"][3]; gt["sbf_sup"] = 0; gt["normread"] = 0
            gt["supd"] = 0; gt["sbf"] = gt["mask"][3]
            continue

        em.wait("vector", sPE, gt["attn_mm"])
        cur, nxt = t % 2, (t + 1) % 2
        _ne = em.op("vector", "tensor_scalar", norm_eps[:, :], P4[:, 0:16], EPS_AT,
                    None, op0=ALU.add, inc=sDV)
        if t >= 1:
            em.wait("vector", sDV, g[t - 1]["supd_s"])
        gt["supd_s"] = em.op("vector", "tensor_add", S_run[nxt][:, :], P56[0:64, :],
                             S_run[cur][:, :], inc=sDV)
        gt["supd"] = em.op("vector", "tensor_add", z_run[nxt][:, :], P4[0:64, 16:32],
                           z_run[cur][:, :], inc=sDV)
        em.wait("vector", sDV, _ne)
        _rc = em.op("vector", "reciprocal", recip[:, :], norm_eps[:, :], inc=sDV)
        em.wait("vector", sDV, gt["supd"])
        em.op("vector", "tensor_copy", S_bf[:, :], S_run[nxt][:, :])
        gt["sbf"] = em.op("vector", "tensor_copy", z_bf[:, :], z_run[nxt][:, :], inc=sDV)
        if debug and t >= 1 and "dbg_spre" in g[t - 1]:
            em.wait("vector", sGP, g[t - 1]["dbg_spre"])
        em.wait("vector", sDV, _rc)
        gt["normread"] = em.op("vector", "tensor_tensor",
                               s_pre[:, :].rearrange("p (hh d) -> p hh d", hh=16),
                               P23.ap().rearrange("p (hh d) -> p hh d", hh=16),
                               recip[:, :].unsqueeze(2).broadcast_to([128, 16, 64]),
                               op=ALU.mult, inc=sDV)

        if b_sub < 2:
            gt["sev"] = gt["normread"]
            gt["sbf_sup"] = 0
            gt["outproj"] = 0
            continue
        if debug:
            em.wait("gpsimd", sDV, gt["normread"])
            gt["dbg_spre"] = em.dma("gpsimd", dbg_spre[t], s_pre[:, :], sGP)
            em.dma("gpsimd", dbg_nrm[t][:, 0:16], norm_eps[:, :], sGP)
            gt["dbg_nrm"] = em.dma("gpsimd", dbg_nrm[t][:, 16:32], recip[:, :], sGP)
        em.wait("tensor", sDV, gt["normread"])
        for j in range(8):
            dst = P0b[:, 128 * j:128 * (j + 1)] if j < 4 else \
                P1b[:, 128 * (j - 4):128 * (j - 3)]
            kw = dict(inc=sPE) if j == 7 else {}
            r = em.op("tensor", "transpose", dst, s_pre[:, 128 * j:128 * (j + 1)],
                      identS[:, :], **kw)
        gt["tr_s"] = r
        em.wait("vector", sPE, gt["tr_s"])
        if debug and "dbg_spre" in gt:
            em.wait("vector", sGP, gt["dbg_spre"])
        em.op("vector", "tensor_copy", sfm[:, 0:512], P0b[:, 0:512])
        gt["sev"] = em.op("vector", "tensor_copy", sfm[:, 512:1024], P1b[:, 0:512],
                          inc=sDV)
        em.wait("tensor", sDV, gt["sev"])
        em.wait("tensor", sGP, M["wout"])
        em.wait("tensor", sSY, M["bias2o"])
        for ncol in range(2):
            for kc in range(8):
                em.op("tensor", "matmul", P23[:, 512 * ncol:512 * (ncol + 1)],
                      sfm[:, 128 * kc:128 * (kc + 1)],
                      Wout[:, E * kc + 512 * ncol:E * kc + 512 * (ncol + 1)],
                      start=(kc == 0), stop=False)
            r = em.op("tensor", "matmul", P23[:, 512 * ncol:512 * (ncol + 1)],
                      ones2S[:, :], bias2o[:, 512 * ncol:512 * (ncol + 1)],
                      start=False, stop=True, inc=sPE)
        gt["outproj"] = r
        em.wait("vector", sPE, gt["outproj"])
        gt["sbf_sup"] = em.op("vector", "tensor_copy",
                              s_sup[:, 1024 * (t % 2):1024 * (t % 2 + 1)],
                              P23.ap(), inc=sDV)
        if debug:
            em.wait("gpsimd", sDV, gt["sbf_sup"])
            em.dma("gpsimd", dbg_s[t], s_sup[:, 1024 * (t % 2):1024 * (t % 2 + 1)], sGP)

        if t % 2 == 1 and b_sub >= 3:
            t0, t1 = 2 * sup, 2 * sup + 1
            em.wait("tensor", sAC, M["wgut"])
            em.wait("tensor", sSY, g[t0]["ld_xnf"])
            em.wait("tensor", sDV, gt["sbf_sup"])
            em.wait("tensor", sDV, gt["supd"])
            em.wait("tensor", sDV, gt["normread"])
            em.wait("tensor", sDV, gt["sev"])
            gt["silu"] = []
            gt["mul"] = []
            gt["down"] = {}
            for q6 in range(6):
                for hcl in range(4):
                    hc = 4 * q6 + hcl
                    pg = (P0, P1)[hc % 2]
                    pu = (P4, P7)[hc % 2]
                    if hc >= 2:
                        em.wait("tensor", sAC, gt["silu"][hc - 2])
                        em.wait("tensor", sDV, gt["mul"][hc - 2])
                    for kc in range(8):
                        em.op("tensor", "matmul", pg[:, 0:256],
                              WgT[:, HID * kc + 128 * hc:HID * kc + 128 * (hc + 1)],
                              xnfM[:, 256 * kc:256 * (kc + 1)],
                              start=(kc == 0), stop=(kc == 7))
                    for kc in range(8):
                        kw = dict(inc=sPE) if kc == 7 else {}
                        r = em.op("tensor", "matmul", pu[:, 0:256],
                                  WuT[:, HID * kc + 128 * hc:HID * kc + 128 * (hc + 1)],
                                  xnfM[:, 256 * kc:256 * (kc + 1)],
                                  start=(kc == 0), stop=(kc == 7), **kw)
                    em.wait("scalar", sPE, r)
                    if hc >= 1:
                        em.wait("scalar", sDV, gt["mul"][hc - 1])
                    sl = em.op("scalar", "activation", silu_t[hc % 2][:, :],
                               pg[:, 0:256], AF.Silu, scale=SC1, inc=sAC)
                    gt["silu"].append(sl)
                    em.wait("vector", sAC, sl)
                    if q6 >= 1:
                        em.wait("vector", sPE, gt["down"][(q6 - 1, hcl)])
                    ml = em.op("vector", "tensor_tensor",
                               hp[:, 256 * hcl:256 * (hcl + 1)],
                               silu_t[hc % 2][:, :], pu[:, 0:256], op=ALU.mult,
                               inc=sDV)
                    gt["mul"].append(ml)
                if b_sub < 4:
                    for hcl in range(4):
                        gt["down"][(q6, hcl)] = 0
                    continue
                em.wait("tensor", sAC, M["wdt"])
                for hcl in range(4):
                    hc = 4 * q6 + hcl
                    em.wait("tensor", sDV, gt["mul"][hc])
                    for tt in range(2):
                        pd = (P23, P56)[tt]
                        for ncol in range(2):
                            kw = dict(inc=sPE) if (ncol == 1 and tt == 1) else {}
                            r = em.op("tensor", "matmul",
                                      pd[:, 512 * ncol:512 * (ncol + 1)],
                                      hp[:, 256 * hcl + 128 * tt:
                                         256 * hcl + 128 * (tt + 1)],
                                      WdT[:, E * hc + 512 * ncol:
                                          E * hc + 512 * (ncol + 1)],
                                      start=(hc == 0), stop=(hc == 23), **kw)
                    gt["down"][(q6, hcl)] = r
            gt["ffn_gu"] = sPE.n
            gt["ffn_mul_last"] = gt["mul"][-1]
            gt["ffn_silu_last"] = gt["silu"][-1]
            gt["down_done"] = sPE.n
            if b_sub < 4:
                continue
            for tt, td in ((0, t0), (1, t1)):
                pd = (P23, P56)[tt]
                em.wait("scalar", sPE, gt["down_done"])
                prev_ov = gt.get("ov_last") if tt == 1 else                     (g[t0 - 1].get("ov_last") if t0 >= 1 else None)
                if prev_ov:
                    em.wait("scalar", sDV, prev_ov)
                gk = em.op("scalar", "activation", g_bf[:, :], pd.ap(), AF.Silu,
                           scale=SC2, inc=sAC)
                g[td]["g_mark"] = gk
                if debug:
                    em.wait("gpsimd", sAC, gk)
                    dv_ = em.dma("gpsimd", dbg_g[td], g_bf[:, :], sGP)
                    em.wait("scalar", sGP, dv_)
                em.wait("vector", sAC, gk)
                for hf in range(2):
                    ob = out_sb[hf]
                    if td >= 1 or tt == 1:
                        em.wait("vector", sGP, g[td - 1]["st_y"] if tt == 0 else 0)
                    prev = gt.get("last_st_y_" + str(hf))
                    if prev:
                        em.wait("vector", sGP, prev)
                    em.wait("vector", sDV, gt["sbf_sup"])
                    ov = em.op("vector", "tensor_tensor", ob[:, :],
                               s_sup[:, 1024 * tt + 512 * hf:1024 * tt + 512 * (hf + 1)],
                               g_bf[:, 512 * hf:512 * (hf + 1)],
                               op=ALU.mult, inc=sDV)
                    gt["ov_last"] = ov
                    g[td]["ov_last"] = ov
                    em.wait("gpsimd", sDV, ov)
                    av = em.dma("gpsimd", ob[:, :], x3[td][:, 512 * hf:512 * (hf + 1)],
                                sGP, accum_op=ALU.add)
                    em.wait("gpsimd", sGP, av)
                    stv = em.dma("gpsimd", y3[td][:, 512 * hf:512 * (hf + 1)], ob[:, :], sGP)
                    gt["last_st_y_" + str(hf)] = stv
                g[td]["st_y"] = stv

    em.wait("gpsimd", sGP, sGP.n)
    return _finish(nc, em)


def _finish(nc, em):
    with nc.Block() as block:
        @block.sync
        def _(eng):
            for f in em.ops["sync"]:
                f(eng)

        @block.gpsimd
        def _(eng):
            for f in em.ops["gpsimd"]:
                f(eng)

        @block.scalar
        def _(eng):
            for f in em.ops["scalar"]:
                f(eng)

        @block.vector
        def _(eng):
            for f in em.ops["vector"]:
                f(eng)

        @block.tensor
        def _(eng):
            for f in em.ops["tensor"]:
                f(eng)

    return nc


_NC_CACHE = {}


def _get_nc(debug=False):
    if debug not in _NC_CACHE:
        _NC_CACHE[debug] = build(debug)
    return _NC_CACHE[debug]


def make_inmaps(L0_attn_x, L0_attn_qkv_w, L0_attn_qkv_b, L0_attn_out_w, L0_attn_out_b,
                L0_ffn_w_gate, L0_ffn_w_up, L0_ffn_w_down):
    bf = ml_dtypes.bfloat16
    ident = np.eye(128, dtype=bf)
    mask4 = np.tile(np.triu(np.ones((128, 128), dtype=bf)), (1, 4))
    ones1 = np.ones((128, 1), dtype=bf)
    ones2 = np.ones((2, 128), dtype=bf)
    wqkv = np.ascontiguousarray(np.asarray(L0_attn_qkv_w, dtype=np.float32))
    bqkv = np.ascontiguousarray(
        np.asarray(L0_attn_qkv_b, dtype=np.float32).reshape(3, E))
    wout = np.ascontiguousarray(np.asarray(L0_attn_out_w, dtype=np.float32))
    bout = np.ascontiguousarray(
        np.asarray(L0_attn_out_b, dtype=np.float32).reshape(1, E))
    wg = np.ascontiguousarray(np.asarray(L0_ffn_w_gate, dtype=np.int32))
    wu = np.ascontiguousarray(np.asarray(L0_ffn_w_up, dtype=np.int32))
    wd = np.ascontiguousarray(np.asarray(L0_ffn_w_down, dtype=np.int32))
    x = np.asarray(L0_attn_x, dtype=np.float32)
    in_maps = []
    for core in range(NCORE):
        b = core // 4
        s0 = (core % 4) * TOK
        mv = np.zeros((64, 4), np.float32)
        mv[:, :core % 4] = 1.0
        in_maps.append({
            "x": np.ascontiguousarray(x[b, s0:s0 + TOK, :]),
            "wqkv": wqkv, "bqkv": bqkv, "wout": wout, "bout": bout,
            "wg": wg, "wu": wu, "wd": wd,
            "ident": ident, "mask4": mask4, "ones1": ones1, "ones2": ones2,
            "maskv": mv,
        })
    return in_maps


def kernel(L0_attn_x, L0_attn_qkv_w, L0_attn_qkv_b, L0_attn_out_w, L0_attn_out_b,
           L0_ffn_w_gate, L0_ffn_w_up, L0_ffn_w_down):
    nc = _get_nc(False)
    in_maps = make_inmaps(L0_attn_x, L0_attn_qkv_w, L0_attn_qkv_b, L0_attn_out_w,
                          L0_attn_out_b, L0_ffn_w_gate, L0_ffn_w_up, L0_ffn_w_down)
    res = run_bass_kernel_spmd(nc, in_maps, core_ids=list(range(NCORE)))
    out = np.empty((B, L, E), dtype=np.float32)
    for core in range(NCORE):
        b = core // 4
        s0 = (core % 4) * TOK
        out[b, s0:s0 + TOK, :] = np.asarray(res.results[core]["y"])
    return out


# version 08
# speedup vs baseline: 1.1529x; 1.0758x over previous
import numpy as np
import ml_dtypes
import concourse.bass as bass
import concourse.mybir as mybir
from concourse.bass_utils import run_bass_kernel_spmd

F32 = mybir.dt.float32
BF16 = mybir.dt.bfloat16
I32 = mybir.dt.int32
U8 = mybir.dt.uint8
AF = mybir.ActivationFunctionType
ALU = mybir.AluOpType
AX = mybir.AxisListType

B, L, E, NH, DH = 2, 8192, 1024, 16, 64
TOK = 2048
NT = TOK // 128
HID = 3072
NCORE = 8
EPS_LN = 1e-5
EPS_AT = 1e-6
SC1 = 1.0 / 127.0
SC2 = 1.0 / (127.0 * 127.0)


class Ctr:
    def __init__(self, stack, nc, name):
        self.h = stack.enter_context(nc.semaphore(name))
        self.n = 0


class Em:
    ENGINES = ("sync", "gpsimd", "scalar", "vector", "tensor")

    def __init__(self):
        self.ops = {k: [] for k in self.ENGINES}
        self.lastwait = {k: {} for k in self.ENGINES}

    def wait(self, eng, ctr, val=None):
        v = ctr.n if val is None else val
        if v is None or v <= 0:
            return
        key = id(ctr)
        if self.lastwait[eng].get(key, -1) >= v:
            return
        self.lastwait[eng][key] = v
        h = ctr.h
        self.ops[eng].append(lambda e: e.wait_ge(h, v))

    def dma(self, eng, out, in_, ctr, **kw):
        h = ctr.h
        self.ops[eng].append(lambda e: e.dma_start(out=out, in_=in_, **kw).then_inc(h, 16))
        ctr.n += 16
        return ctr.n

    def op(self, eng, meth, *args, inc=None, **kw):
        if inc is not None:
            h = inc.h
            inc.n += 1
            v = inc.n
            self.ops[eng].append(
                lambda e: getattr(e, meth)(*args, **kw).then_inc(h, 1))
            return v
        self.ops[eng].append(lambda e: getattr(e, meth)(*args, **kw))
        return None

    def raw(self, eng, fn):
        self.ops[eng].append(fn)


def build(debug=False, n_t=NT, phase_limit=3, b_sub=4):
    nc = bass.Bass("TRN2", target_bir_lowering=False, detect_race_conditions=False)
    okind = "ExternalOutput" if debug else "Internal"

    x_d = nc.dram_tensor("x", [n_t * 128, E], F32, kind="ExternalInput").ap()
    wqkv_d = nc.dram_tensor("wqkv", [E, 3 * E], F32, kind="ExternalInput").ap()
    bqkv_d = nc.dram_tensor("bqkv", [3, E], F32, kind="ExternalInput").ap()
    wout_d = nc.dram_tensor("wout", [E, E], F32, kind="ExternalInput").ap()
    bout_d = nc.dram_tensor("bout", [1, E], F32, kind="ExternalInput").ap()
    wg_d = nc.dram_tensor("wg", [HID, E], I32, kind="ExternalInput").ap()
    wu_d = nc.dram_tensor("wu", [HID, E], I32, kind="ExternalInput").ap()
    wd_d = nc.dram_tensor("wd", [E, HID], I32, kind="ExternalInput").ap()
    ident_d = nc.dram_tensor("ident", [128, 128], BF16, kind="ExternalInput").ap()
    mask4_d = nc.dram_tensor("mask4", [128, 512], BF16, kind="ExternalInput").ap()
    ones1_d = nc.dram_tensor("ones1", [128, 1], BF16, kind="ExternalInput").ap()
    ones2_d = nc.dram_tensor("ones2", [2, 128], BF16, kind="ExternalInput").ap()
    maskv_d = nc.dram_tensor("maskv", [64, 4], F32, kind="ExternalInput").ap()

    y_d = nc.dram_tensor("y", [n_t * 128, E], F32, kind="ExternalOutput").ap()

    wgbf_d = nc.dram_tensor("wgbf", [HID, E], BF16, kind="Internal").ap()
    wubf_d = nc.dram_tensor("wubf", [HID, E], BF16, kind="Internal").ap()
    wdbf_d = nc.dram_tensor("wdbf", [E, HID], BF16, kind="Internal").ap()
    b2q_d = nc.dram_tensor("b2q", [2, 3 * E], BF16, kind="Internal").ap()
    b2o_d = nc.dram_tensor("b2o", [2, E], BF16, kind="Internal").ap()
    xnf_d = nc.dram_tensor("xnf", [n_t // 2, 8, 128, 256], BF16, kind=okind).ap()
    qk_d = nc.dram_tensor("qfm", [n_t, 128, 2 * E], BF16, kind=okind).ap()
    ktv_d = nc.dram_tensor("ktv", [n_t, 128, 2 * E], BF16, kind=okind).ap()
    ccin_d = nc.dram_tensor("ccin", [64, 1040], F32, kind="Internal").ap()
    ccout_d = nc.dram_tensor("ccout", [256, 1040], F32, kind="Internal").ap()
    if debug:
        dbg_sloc = nc.dram_tensor("dbg_sloc", [64, 1040], F32, kind="ExternalOutput").ap()
        dbg_s0 = nc.dram_tensor("dbg_s0", [64, 1040], F32, kind="ExternalOutput").ap()
        dbg_s = nc.dram_tensor("dbg_s", [n_t, 128, E], BF16, kind="ExternalOutput").ap()
        dbg_g = nc.dram_tensor("dbg_g", [n_t, 128, E], BF16, kind="ExternalOutput").ap()
        dbg_spre = nc.dram_tensor("dbg_spre", [n_t, 128, E], BF16, kind="ExternalOutput").ap()
        dbg_nrm = nc.dram_tensor("dbg_nrm", [n_t, 128, 32], F32, kind="ExternalOutput").ap()

    WgT = nc.alloc_sbuf_tensor("WgT", [128, 8 * HID], BF16)
    WuT = nc.alloc_sbuf_tensor("WuT", [128, 8 * HID], BF16)
    R3 = nc.alloc_sbuf_tensor("R3", [128, 8 * HID], BF16)
    Wqkv = R3
    WdT = R3
    Wout = nc.alloc_sbuf_tensor("Wout", [128, 8 * E], BF16)
    identS = nc.alloc_sbuf_tensor("identS", [128, 128], BF16)
    mask4S = nc.alloc_sbuf_tensor("mask4S", [128, 512], BF16)
    ones1S = nc.alloc_sbuf_tensor("ones1S", [128, 1], BF16)
    ones2S = nc.alloc_sbuf_tensor("ones2S", [2, 128], BF16)
    maskvS = nc.alloc_sbuf_tensor("maskvS", [64, 4], F32)

    UN_BYTES = 47232
    un = nc.alloc_sbuf_tensor("UN", [128, UN_BYTES], U8)
    ub = nc.lookup_mloc(un).addr

    def at(name, shape, dtype, off):
        return nc.alloc_sbuf_tensor_at(name, shape, dtype, offset=ub + off)

    xt = [at("xt0", [128, E], F32, 0), at("xt1", [128, E], F32, 4096)]
    em_t = at("em", [128, 512], BF16, 8192)
    qbf = at("qbf", [128, E], BF16, 9216)
    ktvA = at("ktvA", [128, 2 * E], BF16, 11264)
    qfmA = at("qfmA", [128, E], BF16, 15360)
    kfmA = at("kfmA", [128, E], BF16, 17408)
    etmp = at("etmp", [128, 512], F32, 19456)
    xnbf = at("xnbf", [128, E], BF16, 23552)
    xnfmA = [at("xnfmA0", [128, E], BF16, 25600), at("xnfmA1", [128, E], BF16, 21504)]
    sloc = [at("sloc0", [64, 1040], F32, 27648), at("sloc1", [64, 1040], F32, 31872)]
    bias2q = at("bias2q", [2, 3 * E], BF16, 36096)
    sums = [at(f"sums{i}", [128, 1], F32, 42240 + 384 * i) for i in range(2)]
    negm = [at(f"negm{i}", [128, 1], F32, 42304 + 384 * i) for i in range(2)]
    ssq = [at(f"ssq{i}", [128, 1], F32, 42368 + 384 * i) for i in range(2)]
    lnv = [at(f"lnv{i}", [128, 1], F32, 42432 + 384 * i) for i in range(2)]
    rstd = [at(f"rstd{i}", [128, 1], F32, 42496 + 384 * i) for i in range(2)]
    mbias = [at(f"mbias{i}", [128, 1], F32, 42560 + 384 * i) for i in range(2)]
    stg = [at("stg0", [128, E], BF16, 43008), at("stg1", [128, E], BF16, 45056)]
    bstq = at("bstq", [3, E], F32, 0)
    bsto = at("bsto", [1, E], F32, 4096)
    bhiq = at("bhiq", [3, E], BF16, 8192)
    bloq = at("bloq", [3, E], BF16, 11264)
    bhio = at("bhio", [1, E], BF16, 15360)
    bloo = at("bloo", [1, E], BF16, 17408)

    S_run = [at("Srun0", [64, 1024], F32, 0), at("Srun1", [64, 1024], F32, 4096)]
    S_bf = at("Sbf", [64, 1024], BF16, 8192)
    z_run = [at("zrun0", [64, 16], F32, 10240), at("zrun1", [64, 16], F32, 10304)]
    z_bf = at("zbf", [64, 16], BF16, 10368)
    norm_eps = at("normeps", [128, 16], F32, 10432)
    recip = at("recip", [128, 16], F32, 10496)
    qkM = at("qkM", [64, 4 * E], BF16, 10624)
    ktvM = at("ktvM", [128, 2 * E], BF16, 18816)
    at_bf2 = [at("atbf0", [128, 512], BF16, 22912), at("atbf1", [128, 512], BF16, 37248)]
    s_pre = at("spre", [128, E], BF16, 23936)
    sfm = at("sfm", [128, E], BF16, 23936)
    s_sup = at("ssup", [128, 2 * E], BF16, 25984)
    xnfM = at("xnfM", [128, 2048], BF16, 30080)
    hp = at("hp", [128, 1024], BF16, 34176)
    silu_t = [at("silu0", [128, 256], F32, 36224), at("silu1", [128, 256], F32, 36224)]
    g_bf = at("gbf", [128, E], BF16, 38272)
    out_sb = [at("outsb0", [128, 512], F32, 40320), at("outsb1", [128, 512], F32, 42368)]
    bias2o = at("bias2o", [2, E], BF16, 44416)
    gb = [at("gb0", [64, 1040], F32, 23936), at("gb1", [64, 1040], F32, 32256)]
    gterm = at("gterm", [64, 1040], F32, 38272)
    gacc = [at("gacc0", [64, 1040], F32, 42432), at("gacc1", [64, 1040], F32, 28096)]

    P0 = nc.alloc_psum_tensor("P0", [128, 512], F32)
    P1 = nc.alloc_psum_tensor("P1", [128, 512], F32)
    P23 = nc.alloc_psum_tensor("P23", [128, 1024], F32)
    P4 = nc.alloc_psum_tensor("P4", [128, 512], F32)
    P56 = nc.alloc_psum_tensor("P56", [128, 1024], F32)
    P7 = nc.alloc_psum_tensor("P7", [128, 512], F32)
    P0b = P0.ap().bitcast(BF16)
    P1b = P1.ap().bitcast(BF16)
    P23b = P23.ap().bitcast(BF16)

    from contextlib import ExitStack
    stack = ExitStack()
    sSY = Ctr(stack, nc, "sSY")
    sGP = Ctr(stack, nc, "sGP")
    sPE = Ctr(stack, nc, "sPE")
    sDV = Ctr(stack, nc, "sDV")
    sAC = Ctr(stack, nc, "sAC")
    sCC = Ctr(stack, nc, "sCC")

    em = Em()
    M = {}

    em.dma("sync", identS[:, :], ident_d, sSY)
    em.dma("sync", mask4S[:, :], mask4_d, sSY)
    em.dma("sync", ones1S[:, :], ones1_d, sSY)
    em.dma("sync", ones2S[:, :], ones2_d, sSY)
    em.dma("sync", maskvS[:, :], maskv_d, sSY)
    em.dma("sync", bstq[:, :], bqkv_d, sSY)
    em.dma("sync", bsto[:, :], bout_d, sSY)
    M["consts"] = sSY.n

    wq3 = wqkv_d.rearrange("(c p) n -> c p n", p=128)
    for c in range(8):
        em.dma("gpsimd", Wqkv[:, c * 3 * E:(c + 1) * 3 * E], wq3[c], sGP)
    M["wqkv"] = sGP.n
    wo3 = wout_d.rearrange("(c p) n -> c p n", p=128)
    for c in range(8):
        em.dma("gpsimd", Wout[:, c * E:(c + 1) * E], wo3[c], sGP)
    M["wout"] = sGP.n

    em.wait("vector", sSY, M["consts"])
    _bq = em.op("vector", "tensor_copy", bhiq[:, :], bstq[:, :], inc=sDV)
    em.wait("vector", sDV, _bq)
    em.op("vector", "tensor_tensor", bloq[:, :], bstq[:, :], bhiq[:, :], op=ALU.subtract)
    _bo = em.op("vector", "tensor_copy", bhio[:, :], bsto[:, :], inc=sDV)
    em.wait("vector", sDV, _bo)
    M["biasdv"] = em.op("vector", "tensor_tensor", bloo[:, :], bsto[:, :], bhio[:, :],
                        op=ALU.subtract, inc=sDV)
    em.wait("gpsimd", sDV, M["biasdv"])
    em.dma("gpsimd", b2q_d[0:1, :].rearrange("a (b c) -> (a b) c", b=3), bhiq[:, :], sGP)
    em.dma("gpsimd", b2q_d[1:2, :].rearrange("a (b c) -> (a b) c", b=3), bloq[:, :], sGP)
    em.dma("gpsimd", b2o_d[0:1, :], bhio[:, :], sGP)
    em.dma("gpsimd", b2o_d[1:2, :], bloo[:, :], sGP)
    M["biasbounce"] = sGP.n
    em.wait("sync", sGP, M["biasbounce"])
    em.dma("sync", bias2q[:, :], b2q_d, sSY)
    M["bias2"] = sSY.n

    def load_act_table(set_name):
        def f(e):
            from concourse.hw_specs import get_activation_tables
            names = list(get_activation_tables(nc.m.arch).keys())
            e.add_instruction(mybir.InstLoadActFuncSet(
                name=nc.get_next_instruction_name(),
                act_func_set_id=names.index(set_name), ins=[], outs=[]))
        em.raw("scalar", f)

    load_act_table("natural_log_exp_and_others")

    bounce_jobs = []
    for src, dst in ((wg_d, wgbf_d), (wu_d, wubf_d), (wd_d, wdbf_d)):
        s4 = src.rearrange("(t p) (c d) -> t c p d", p=128, d=1024)
        d4 = dst.rearrange("(t p) (c d) -> t c p d", p=128, d=1024)
        for t in range(s4.shape[0]):
            for c in range(s4.shape[1]):
                bounce_jobs.append((s4[t, c], d4[t, c]))
    bstate = {"i": 0, "hist": []}

    def emit_bounce(n):
        for _ in range(n):
            i = bstate["i"]
            if i >= len(bounce_jobs):
                return
            src, dst = bounce_jobs[i]
            b = stg[i % 2]
            if i >= 2:
                em.wait("gpsimd", sGP, bstate["hist"][i - 2][1])
            ldv = em.dma("gpsimd", b[:, :], src, sGP)
            em.wait("gpsimd", sGP, ldv)
            stv = em.dma("gpsimd", dst, b[:, :], sGP)
            bstate["hist"].append((ldv, stv))
            bstate["i"] += 1

    x3 = x_d.rearrange("(t p) d -> t p d", p=128)
    y3 = y_d.rearrange("(t p) d -> t p d", p=128)

    if phase_limit < 1:
        em.wait("gpsimd", sGP, sGP.n)
        em.wait("sync", sSY, sSY.n)
        return _finish(nc, em)

    h = [dict() for _ in range(n_t)]

    def emit_xload(t):
        ht = h[t]
        if t == 0:
            em.wait("sync", sDV, M["biasdv"])
        if t >= 2:
            em.wait("sync", sAC, h[t - 2]["xnbf"])
        ht["x"] = em.dma("sync", xt[t % 2][:, :], x3[t], sSY)

    def emit_ln(t):
        ht = h[t]
        p2 = t % 2
        em.wait("vector", sSY, ht["x"])
        if t == 0:
            em.wait("vector", sGP, M["biasbounce"])
        _rv = em.op("vector", "reduce_sum", sums[p2][:, :], xt[t % 2][:, :],
                    axis=AX.X, inc=sDV)
        em.wait("vector", sDV, _rv)
        ht["negm"] = em.op("vector", "tensor_scalar", negm[p2][:, :], sums[p2][:, :],
                           -1.0 / E, None, op0=ALU.mult, inc=sDV)
        em.wait("scalar", sSY, ht["x"])
        em.wait("scalar", sDV, ht["negm"])
        if t >= 1:
            em.wait("scalar", sPE, h[t - 1]["tr_xn"])
        sqv = em.op("scalar", "activation", xnbf[:, :], xt[t % 2][:, :], AF.Square,
                    bias=negm[p2][:, :], accum_out=ssq[p2][:, :], inc=sAC)
        em.wait("vector", sAC, sqv)
        veps = em.op("vector", "tensor_scalar", lnv[p2][:, :], ssq[p2][:, :], 1.0 / E,
                     EPS_LN, op0=ALU.mult, op1=ALU.add, inc=sDV)
        em.wait("scalar", sDV, veps)
        em.op("scalar", "activation", lnv[p2][:, :], lnv[p2][:, :], AF.Ln)
        ht["rstd"] = em.op("scalar", "activation", rstd[p2][:, :], lnv[p2][:, :],
                           AF.Exp, scale=-0.5, inc=sAC)
        em.wait("vector", sAC, ht["rstd"])
        em.wait("vector", sDV, ht["negm"])
        ht["mb"] = em.op("vector", "tensor_scalar", mbias[p2][:, :], negm[p2][:, :],
                         rstd[p2][:, :], None, op0=ALU.mult, inc=sDV)
        em.wait("scalar", sDV, ht["mb"])
        ht["xnbf"] = em.op("scalar", "activation", xnbf[:, :], xt[t % 2][:, :],
                           AF.Identity, bias=mbias[p2][:, :], scale=rstd[p2][:, :],
                           inc=sAC)

    def emit_trxn(t):
        ht = h[t]
        em.wait("tensor", sSY, M["consts"])
        em.wait("tensor", sAC, ht["xnbf"])
        if t >= 1:
            em.wait("tensor", sDV, h[t - 1]["kfm_ev"])
        for j in range(8):
            kw = dict(inc=sPE) if j == 7 else {}
            r = em.op("tensor", "transpose", P23b[:, 128 * j:128 * (j + 1)],
                      xnbf[:, 128 * j:128 * (j + 1)], identS[:, :], **kw)
        ht["tr_xn"] = r
        em.wait("vector", sPE, ht["tr_xn"])
        xnA = xnfmA[t % 2]
        if t >= 2:
            em.wait("vector", sGP, h[t - 2]["st_xnf"])
            em.wait("vector", sPE, h[t - 2]["qkv_done"])
        ht["xnfm_ev"] = em.op("vector", "tensor_copy", xnA[:, :], P23b[:, 0:1024],
                              inc=sDV)
        em.wait("gpsimd", sDV, ht["xnfm_ev"])
        dst = xnf_d[t // 2][:, :, (t % 2) * 128:(t % 2) * 128 + 128].rearrange(
            "k p u -> p k u")
        ht["st_xnf"] = em.dma("gpsimd", dst,
                              xnA[:, :].rearrange("p (k u) -> p k u", k=8), sGP)

    def emit_qkv(t):
        ht = h[t]
        em.wait("tensor", sGP, M["wqkv"])
        em.wait("tensor", sSY, M["bias2"])
        em.wait("tensor", sDV, ht["xnfm_ev"])
        ht["elu"] = [None] * 6
        ht["qkv_cc"] = []
        for cc in range(6):
            pb = (P0, P1, P7)[cc % 3]
            if cc >= 3:
                em.wait("tensor", sDV, ht["elu"][cc - 3])
            elif t >= 1:
                em.wait("tensor", sDV, h[t - 1]["elu"][cc + 3])
            for kc in range(8):
                em.op("tensor", "matmul", pb[:, :],
                      xnfmA[t % 2][:, 128 * kc:128 * (kc + 1)],
                      Wqkv[:, 3 * E * kc + 512 * cc:3 * E * kc + 512 * (cc + 1)],
                      start=(kc == 0), stop=False)
            v = em.op("tensor", "matmul", pb[:, :], ones2S[:, :],
                      bias2q[:, 512 * cc:512 * (cc + 1)], start=False, stop=True,
                      inc=sPE)
            ht["qkv_cc"].append(v)
            if cc < 4:
                em.wait("scalar", sPE, v)
                if cc >= 1:
                    em.wait("scalar", sDV, ht["elu"][cc - 1])
                elif t >= 1:
                    em.wait("scalar", sDV, h[t - 1]["elu"][3])
                ev = em.op("scalar", "activation", etmp[:, :], pb[:, :], AF.Exp,
                           inc=sAC)
                em.wait("vector", sAC, ev)
                _mv = em.op("vector", "tensor_scalar", em_t[:, :], etmp[:, :], 1.0,
                            None, op0=ALU.min, inc=sDV)
                em.wait("vector", sDV, _mv)
                if cc < 2:
                    dest = qbf[:, 512 * cc:512 * (cc + 1)]
                    if t >= 1:
                        em.wait("vector", sPE, h[t - 1]["tr_q"])
                else:
                    dest = ktvA[:, 512 * (cc - 2):512 * (cc - 1)]
                    if t >= 1:
                        em.wait("vector", sPE, h[t - 1]["ds"])
                        em.wait("vector", sSY, h[t - 1]["st_ktv"])
                ht["elu"][cc] = em.op("vector", "scalar_tensor_tensor", dest,
                                      pb[:, :], 0.0, em_t[:, :],
                                      op0=ALU.max, op1=ALU.add, inc=sDV)
            else:
                em.wait("vector", sPE, v)
                if t >= 1:
                    em.wait("vector", sPE, h[t - 1]["ds"])
                    em.wait("vector", sSY, h[t - 1]["st_ktv"])
                dest = ktvA[:, 1024 + 512 * (cc - 4):1024 + 512 * (cc - 3)]
                ht["elu"][cc] = em.op("vector", "tensor_copy", dest, pb[:, :], inc=sDV)
        ht["qkv_done"] = ht["qkv_cc"][5]

    def emit_trqk(t):
        ht = h[t]
        em.wait("tensor", sDV, ht["elu"][1])
        if t >= 1:
            em.wait("tensor", sDV, h[t - 1]["qfm_ev"])
        for j in range(8):
            kw = dict(inc=sPE) if j == 7 else {}
            r = em.op("tensor", "transpose",
                      P23b[:, 1024 + 128 * j:1024 + 128 * (j + 1)],
                      qbf[:, 128 * j:128 * (j + 1)], identS[:, :], **kw)
        ht["tr_q"] = r
        em.wait("tensor", sDV, ht["elu"][3])
        em.wait("tensor", sDV, ht["xnfm_ev"])
        for j in range(8):
            kw = dict(inc=sPE) if j == 7 else {}
            r = em.op("tensor", "transpose", P23b[:, 128 * j:128 * (j + 1)],
                      ktvA[:, 128 * j:128 * (j + 1)], identS[:, :], **kw)
        ht["tr_k"] = r
        em.wait("vector", sPE, ht["tr_q"])
        if t >= 1:
            em.wait("vector", sGP, h[t - 1]["st_qfm"])
        ht["qfm_ev"] = em.op("vector", "tensor_copy", qfmA[:, :], P23b[:, 1024:2048],
                             inc=sDV)
        em.wait("vector", sPE, ht["tr_k"])
        if t >= 1:
            em.wait("vector", sGP, h[t - 1]["st_kfm"])
        ht["kfm_ev"] = em.op("vector", "tensor_copy", kfmA[:, :], P23b[:, 0:1024],
                             inc=sDV)
        em.wait("gpsimd", sDV, ht["qfm_ev"])
        em.dma("gpsimd", qk_d[t][:, 0:1024], qfmA[:, :], sGP)
        em.wait("gpsimd", sDV, ht["kfm_ev"])
        ht["st_qfm"] = em.dma("gpsimd", qk_d[t][:, 1024:2048], kfmA[:, :], sGP)
        ht["st_kfm"] = ht["st_qfm"]
        em.wait("sync", sDV, ht["elu"][5])
        ht["st_ktv"] = em.dma("sync", ktv_d[t], ktvA[:, :], sSY)

    def emit_ds(t):
        ht = h[t]
        em.wait("tensor", sDV, ht["elu"][3])
        em.wait("tensor", sDV, ht["elu"][5])
        if t >= 1:
            em.wait("tensor", sDV, h[t - 1]["sloc"])
        for hh in range(NH):
            em.op("tensor", "matmul", P56[0:64, 64 * hh:64 * (hh + 1)],
                  ktvA[:, 64 * hh:64 * (hh + 1)],
                  ktvA[:, 1024 + 64 * hh:1024 + 64 * (hh + 1)], start=True, stop=True)
            kw = dict(inc=sPE) if hh == NH - 1 else {}
            r = em.op("tensor", "matmul", P4[0:64, 16 + hh:17 + hh],
                      ktvA[:, 64 * hh:64 * (hh + 1)], ones1S[:, :],
                      start=True, stop=True, **kw)
        ht["ds"] = r

    def emit_sloc(t):
        ht = h[t]
        em.wait("vector", sPE, ht["ds"])
        cur, prv = sloc[t % 2], sloc[(t + 1) % 2]
        if t == 0:
            em.op("vector", "tensor_copy", cur[:, 0:1024], P56[0:64, :], inc=sDV)
            r = em.op("vector", "tensor_copy", cur[:, 1024:1040], P4[0:64, 16:32],
                      inc=sDV)
        else:
            em.wait("vector", sDV, h[t - 1]["sloc"])
            em.op("vector", "tensor_add", cur[:, 0:1024], P56[0:64, :],
                  prv[:, 0:1024], inc=sDV)
            r = em.op("vector", "tensor_add", cur[:, 1024:1040], P4[0:64, 16:32],
                      prv[:, 1024:1040], inc=sDV)
        ht["sloc"] = r

    emit_xload(0)
    emit_ln(0)
    for t in range(n_t):
        if t + 1 < n_t:
            emit_xload(t + 1)
        emit_trxn(t)
        if t >= 1:
            emit_ds(t - 1)
            emit_sloc(t - 1)
        if t + 1 < n_t:
            emit_ln(t + 1)
        emit_qkv(t)
        emit_trqk(t)
        emit_bounce(5)
    emit_ds(n_t - 1)
    emit_sloc(n_t - 1)

    emit_bounce(len(bounce_jobs))
    M["bounce_wg"] = bstate["hist"][23][1]
    M["bounce_wu"] = bstate["hist"][47][1]
    M["bounce_wd"] = bstate["hist"][71][1]

    if phase_limit < 2:
        em.wait("gpsimd", sGP, sGP.n)
        return _finish(nc, em)
    em.wait("gpsimd", sDV, h[n_t - 1]["sloc"])
    slocF = sloc[(n_t - 1) % 2]
    ccin_st = em.dma("gpsimd", ccin_d, slocF[:, :], sGP)
    em.wait("gpsimd", sGP, ccin_st)
    ccv = em.op("gpsimd", "collective_compute", "AllGather", ALU.bypass,
                replica_groups=[[0, 1, 2, 3], [4, 5, 6, 7]],
                ins=[ccin_d.opt()], outs=[ccout_d.opt()], inc=sCC)
    if debug:
        em.dma("gpsimd", dbg_sloc, ccin_d, sGP)

    em.wait("scalar", sGP, M["bounce_wg"])
    for dc in range(8):
        em.dma("scalar", WgT[:, HID * dc:HID * (dc + 1)],
               wgbf_d[:, 128 * dc:128 * (dc + 1)], sAC, transpose=True)
    em.wait("scalar", sGP, M["bounce_wu"])
    for dc in range(8):
        em.dma("scalar", WuT[:, HID * dc:HID * (dc + 1)],
               wubf_d[:, 128 * dc:128 * (dc + 1)], sAC, transpose=True)
    M["wgut"] = sAC.n
    em.wait("scalar", sPE, h[n_t - 1]["qkv_done"])
    em.wait("scalar", sGP, M["bounce_wd"])
    for hc in range(24):
        em.dma("scalar", WdT[:, E * hc:E * (hc + 1)],
               wdbf_d[:, 128 * hc:128 * (hc + 1)], sAC, transpose=True)
    M["wdt"] = sAC.n
    cco = ccout_d.rearrange("(r p) f -> r p f", p=64)
    prevmul = None
    for r in range(4):
        em.wait("sync", sCC, ccv)
        if r >= 2:
            em.wait("sync", sDV, M[f"gmul{r - 2}"])
        gl = em.dma("sync", gb[r % 2][:, :], cco[r], sSY)
        em.wait("vector", sSY, gl)
        if r == 0:
            M["gmul0"] = em.op("vector", "tensor_scalar", gacc[0][:, :], gb[0][:, :],
                               maskvS[:, 0:1], None, op0=ALU.mult, inc=sDV)
            M["gacc0"] = M["gmul0"]
        else:
            M[f"gmul{r}"] = em.op("vector", "tensor_scalar", gterm[:, :],
                                  gb[r % 2][:, :], maskvS[:, r:r + 1], None,
                                  op0=ALU.mult, inc=sDV)
            em.wait("vector", sDV, M[f"gmul{r}"])
            em.wait("vector", sDV, M[f"gacc{r - 1}"])
            M[f"gacc{r}"] = em.op("vector", "tensor_add", gacc[r % 2][:, :],
                                  gterm[:, :], gacc[(r - 1) % 2][:, :], inc=sDV)
    s0 = gacc[1]
    if debug:
        M["s0pre"] = em.op("vector", "engine_nop", inc=sDV)
        em.wait("gpsimd", sDV, M["s0pre"])
        em.dma("gpsimd", dbg_s0, s0[:, :], sGP)
    em.wait("vector", sDV, M["gacc3"])
    _c1 = em.op("vector", "tensor_copy", S_run[0][:, :], s0[:, 0:1024], inc=sDV)
    _c2 = em.op("vector", "tensor_copy", z_run[0][:, :], s0[:, 1024:1040], inc=sDV)
    em.wait("vector", sDV, _c2)
    em.op("vector", "tensor_copy", S_bf[:, :], S_run[0][:, :])
    M["sbf0"] = em.op("vector", "tensor_copy", z_bf[:, :], z_run[0][:, :], inc=sDV)
    M["gapdv"] = M["sbf0"]
    em.wait("sync", sGP, M["bounce_wd"])
    em.wait("sync", sDV, M["gapdv"])
    em.dma("sync", bias2o[:, :], b2o_d, sSY)
    M["bias2o"] = sSY.n
    M["allst"] = sGP.n
    M["allstS"] = sSY.n

    load_act_table("silu_and_others")

    if phase_limit < 3:
        em.wait("gpsimd", sGP, sGP.n)
        em.wait("vector", sDV, sDV.n)
        return _finish(nc, em)

    def head_blk(tens, hh, w):
        return tens[0:64, w * hh:w * (hh + 1)]

    g = [dict() for _ in range(n_t)]

    def emit_ldtile(t):
        gt = g[t]
        sup = t // 2
        em.wait("sync", sGP, M["allst"])
        em.wait("sync", sSY, M["allstS"])
        em.wait("gpsimd", sGP, M["allst"])
        if t >= 1:
            em.wait("sync", sPE, g[t - 1]["attn_mm"])
            em.wait("gpsimd", sPE, g[t - 1]["attn_mm"])
        qm3 = qkM[0:64, :].rearrange("p (half j u) -> p half j u", half=2, u=128)
        qs3 = qk_d[t].rearrange("p (half u) -> p half u", half=2)
        em.dma("gpsimd", qm3[:, :, 0::2, :], qs3[0:64], sGP)
        gt["ld_q"] = em.dma("gpsimd", qm3[:, :, 1::2, :], qs3[64:128], sGP)
        gt["ld_k"] = gt["ld_q"]
        gt["ld_v"] = em.dma("sync", ktvM[:, :], ktv_d[t], sSY)

    def emit_ldxnf(sup):
        t = 2 * sup
        em.wait("sync", sDV, M["gapdv"])
        if sup >= 1:
            em.wait("sync", sPE, g[t - 1]["ffn_gu"])
        g[t]["ld_xnf"] = em.dma("sync", xnfM[:, :].rearrange("p (k u) -> p k u", k=8),
                                xnf_d[sup].rearrange("k p u -> p k u"), sSY)

    for t in range(n_t):
        gt = g[t]
        sup = t // 2
        if t == 0:
            emit_ldtile(0)
            emit_ldxnf(0)

        if b_sub == 11:
            gt["attn_mm"] = 0
            gt["sev"] = 0; gt["sbf_sup"] = 0; gt["normread"] = 0
            gt["supd"] = 0; gt["sbf"] = 0; gt["heads"] = [0, 0, 0, 0]
            continue
        em.wait("tensor", sGP, gt["ld_q"])
        gt["atp"] = []
        gt["mask"] = []
        gt["heads"] = []

        def at_pass(p):
            pb = (P0, P1)[p % 2]
            if p >= 2:
                em.wait("tensor", sDV, gt["mask"][p - 2])
            elif t >= 1:
                em.wait("tensor", sDV, g[t - 1]["sev"])
                if "ffn_silu_last" in g[t - 1]:
                    em.wait("tensor", sAC, g[t - 1]["ffn_silu_last"])
            for hi in range(4):
                hh = 4 * p + hi
                kw = dict(inc=sPE) if hi == 3 else {}
                r = em.op("tensor", "matmul", pb[:, 128 * hi:128 * (hi + 1)],
                          qkM[0:64, 2048 + 128 * hh:2176 + 128 * hh], qkM[0:64, 128 * hh:128 * (hh + 1)],
                          start=True, stop=True, **kw)
            gt["atp"].append(r)

        def mask_pass(p):
            pb = (P0, P1)[p % 2]
            ab = at_bf2[p % 2]
            em.wait("vector", sPE, gt["atp"][p])
            if p >= 2:
                em.wait("vector", sPE, gt["heads"][p - 2])
            elif t >= 1:
                em.wait("vector", sPE, g[t - 1]["heads"][2 + p])
            mk = em.op("vector", "tensor_tensor", ab[:, :], pb[:, :], mask4S[:, :],
                       op=ALU.mult, inc=sDV)
            gt["mask"].append(mk)

        def head_pass(p):
            em.wait("tensor", sDV, gt["mask"][p])
            em.wait("tensor", sDV, M["sbf0"] if t == 0 else g[t - 1]["sbf"])
            em.wait("tensor", sSY, gt["ld_v"])
            if p == 0:
                if t >= 1:
                    em.wait("tensor", sDV, g[t - 1]["sbf_sup"])
                    em.wait("tensor", sDV, g[t - 1]["normread"])
                    em.wait("tensor", sDV, g[t - 1]["supd"])
                if sup >= 1 and t % 2 == 0:
                    em.wait("tensor", sAC, g[t - 2]["g_mark"])
                    em.wait("tensor", sAC, g[t - 1]["g_mark"])
                    em.wait("tensor", sDV, g[t - 1]["ffn_mul_last"])
            ab = at_bf2[p % 2]
            for hi in range(4):
                hh = 4 * p + hi
                em.op("tensor", "matmul", P23[:, 64 * hh:64 * (hh + 1)],
                      ab[:, 128 * hi:128 * (hi + 1)],
                      ktvM[:, 1024 + 64 * hh:1024 + 64 * (hh + 1)],
                      start=True, stop=False)
                em.op("tensor", "matmul", P23[:, 64 * hh:64 * (hh + 1)],
                      qkM[0:64, 128 * hh:128 * (hh + 1)], head_blk(S_bf, hh, 64),
                      start=False, stop=True)
                em.op("tensor", "matmul", P4[:, hh:hh + 1],
                      ab[:, 128 * hi:128 * (hi + 1)], ones1S[:, :],
                      start=True, stop=False)
                em.op("tensor", "matmul", P4[:, hh:hh + 1],
                      qkM[0:64, 128 * hh:128 * (hh + 1)], head_blk(z_bf, hh, 1),
                      start=False, stop=True)
                em.op("tensor", "matmul", P56[0:64, 64 * hh:64 * (hh + 1)],
                      ktvM[:, 64 * hh:64 * (hh + 1)],
                      ktvM[:, 1024 + 64 * hh:1024 + 64 * (hh + 1)],
                      start=True, stop=True)
                kw = dict(inc=sPE) if hi == 3 else {}
                r = em.op("tensor", "matmul", P4[0:64, 16 + hh:17 + hh],
                          ktvM[:, 64 * hh:64 * (hh + 1)], ones1S[:, :],
                          start=True, stop=True, **kw)
            gt["heads"].append(r)

        if b_sub == 12:
            for p in range(4):
                at_pass(p)
                mask_pass(p)
                gt["heads"].append(gt["atp"][p])
            gt["attn_mm"] = gt["atp"][3]
            gt["sev"] = gt["mask"][3]; gt["sbf_sup"] = 0; gt["normread"] = 0
            gt["supd"] = 0; gt["sbf"] = gt["mask"][3]
            continue
        at_pass(0)
        mask_pass(0)
        at_pass(1)
        head_pass(0)
        mask_pass(1)
        at_pass(2)
        head_pass(1)
        mask_pass(2)
        at_pass(3)
        head_pass(2)
        mask_pass(3)
        head_pass(3)
        gt["attn_mm"] = gt["heads"][3]
        if t + 1 < n_t:
            emit_ldtile(t + 1)
        if b_sub == 13:
            gt["sev"] = gt["mask"][3]; gt["sbf_sup"] = 0; gt["normread"] = 0
            gt["supd"] = 0; gt["sbf"] = gt["mask"][3]
            continue

        em.wait("vector", sPE, gt["attn_mm"])
        cur, nxt = t % 2, (t + 1) % 2
        _ne = em.op("vector", "tensor_scalar", norm_eps[:, :], P4[:, 0:16], EPS_AT,
                    None, op0=ALU.add, inc=sDV)
        if t >= 1:
            em.wait("vector", sDV, g[t - 1]["supd_s"])
        gt["supd_s"] = em.op("vector", "tensor_add", S_run[nxt][:, :], P56[0:64, :],
                             S_run[cur][:, :], inc=sDV)
        gt["supd"] = em.op("vector", "tensor_add", z_run[nxt][:, :], P4[0:64, 16:32],
                           z_run[cur][:, :], inc=sDV)
        em.wait("vector", sDV, _ne)
        _rc = em.op("vector", "reciprocal", recip[:, :], norm_eps[:, :], inc=sDV)
        em.wait("vector", sDV, gt["supd"])
        em.op("vector", "tensor_copy", S_bf[:, :], S_run[nxt][:, :])
        gt["sbf"] = em.op("vector", "tensor_copy", z_bf[:, :], z_run[nxt][:, :], inc=sDV)
        if debug and t >= 1 and "dbg_spre" in g[t - 1]:
            em.wait("vector", sGP, g[t - 1]["dbg_spre"])
        em.wait("vector", sDV, _rc)
        gt["normread"] = em.op("vector", "tensor_tensor",
                               s_pre[:, :].rearrange("p (hh d) -> p hh d", hh=16),
                               P23.ap().rearrange("p (hh d) -> p hh d", hh=16),
                               recip[:, :].unsqueeze(2).broadcast_to([128, 16, 64]),
                               op=ALU.mult, inc=sDV)

        if b_sub < 2:
            gt["sev"] = gt["normread"]
            gt["sbf_sup"] = 0
            gt["outproj"] = 0
            continue
        if debug:
            em.wait("gpsimd", sDV, gt["normread"])
            gt["dbg_spre"] = em.dma("gpsimd", dbg_spre[t], s_pre[:, :], sGP)
            em.dma("gpsimd", dbg_nrm[t][:, 0:16], norm_eps[:, :], sGP)
            gt["dbg_nrm"] = em.dma("gpsimd", dbg_nrm[t][:, 16:32], recip[:, :], sGP)
        em.wait("tensor", sDV, gt["normread"])
        for j in range(8):
            dst = P0b[:, 128 * j:128 * (j + 1)] if j < 4 else \
                P1b[:, 128 * (j - 4):128 * (j - 3)]
            kw = dict(inc=sPE) if j == 7 else {}
            r = em.op("tensor", "transpose", dst, s_pre[:, 128 * j:128 * (j + 1)],
                      identS[:, :], **kw)
        gt["tr_s"] = r
        em.wait("vector", sPE, gt["tr_s"])
        if debug and "dbg_spre" in gt:
            em.wait("vector", sGP, gt["dbg_spre"])
        em.op("vector", "tensor_copy", sfm[:, 0:512], P0b[:, 0:512])
        gt["sev"] = em.op("vector", "tensor_copy", sfm[:, 512:1024], P1b[:, 0:512],
                          inc=sDV)
        em.wait("tensor", sDV, gt["sev"])
        em.wait("tensor", sGP, M["wout"])
        em.wait("tensor", sSY, M["bias2o"])
        for ncol in range(2):
            for kc in range(8):
                em.op("tensor", "matmul", P23[:, 512 * ncol:512 * (ncol + 1)],
                      sfm[:, 128 * kc:128 * (kc + 1)],
                      Wout[:, E * kc + 512 * ncol:E * kc + 512 * (ncol + 1)],
                      start=(kc == 0), stop=False)
            r = em.op("tensor", "matmul", P23[:, 512 * ncol:512 * (ncol + 1)],
                      ones2S[:, :], bias2o[:, 512 * ncol:512 * (ncol + 1)],
                      start=False, stop=True, inc=sPE)
        gt["outproj"] = r
        em.wait("vector", sPE, gt["outproj"])
        gt["sbf_sup"] = em.op("vector", "tensor_copy",
                              s_sup[:, 1024 * (t % 2):1024 * (t % 2 + 1)],
                              P23.ap(), inc=sDV)
        if debug:
            em.wait("gpsimd", sDV, gt["sbf_sup"])
            em.dma("gpsimd", dbg_s[t], s_sup[:, 1024 * (t % 2):1024 * (t % 2 + 1)], sGP)

        if t % 2 == 1 and b_sub >= 3:
            t0, t1 = 2 * sup, 2 * sup + 1
            em.wait("tensor", sAC, M["wgut"])
            em.wait("tensor", sSY, g[t0]["ld_xnf"])
            em.wait("tensor", sDV, gt["sbf_sup"])
            em.wait("tensor", sDV, gt["supd"])
            em.wait("tensor", sDV, gt["normread"])
            em.wait("tensor", sDV, gt["sev"])
            gt["silu"] = []
            gt["mul"] = []
            gt["down"] = {}
            for q6 in range(6):
                for hcl in range(4):
                    hc = 4 * q6 + hcl
                    pg = (P0, P1)[hc % 2]
                    pu = (P4, P7)[hc % 2]
                    if hc >= 2:
                        em.wait("tensor", sAC, gt["silu"][hc - 2])
                        em.wait("tensor", sDV, gt["mul"][hc - 2])
                    for kc in range(8):
                        em.op("tensor", "matmul", pg[:, 0:256],
                              WgT[:, HID * kc + 128 * hc:HID * kc + 128 * (hc + 1)],
                              xnfM[:, 256 * kc:256 * (kc + 1)],
                              start=(kc == 0), stop=(kc == 7))
                    for kc in range(8):
                        kw = dict(inc=sPE) if kc == 7 else {}
                        r = em.op("tensor", "matmul", pu[:, 0:256],
                                  WuT[:, HID * kc + 128 * hc:HID * kc + 128 * (hc + 1)],
                                  xnfM[:, 256 * kc:256 * (kc + 1)],
                                  start=(kc == 0), stop=(kc == 7), **kw)
                    em.wait("scalar", sPE, r)
                    if hc >= 1:
                        em.wait("scalar", sDV, gt["mul"][hc - 1])
                    sl = em.op("scalar", "activation", silu_t[hc % 2][:, :],
                               pg[:, 0:256], AF.Silu, scale=SC1, inc=sAC)
                    gt["silu"].append(sl)
                    em.wait("vector", sAC, sl)
                    if q6 >= 1:
                        em.wait("vector", sPE, gt["down"][(q6 - 1, hcl)])
                    ml = em.op("vector", "tensor_tensor",
                               hp[:, 256 * hcl:256 * (hcl + 1)],
                               silu_t[hc % 2][:, :], pu[:, 0:256], op=ALU.mult,
                               inc=sDV)
                    gt["mul"].append(ml)
                if b_sub < 4:
                    for hcl in range(4):
                        gt["down"][(q6, hcl)] = 0
                    continue
                em.wait("tensor", sAC, M["wdt"])
                for hcl in range(4):
                    hc = 4 * q6 + hcl
                    em.wait("tensor", sDV, gt["mul"][hc])
                    for tt in range(2):
                        pd = (P23, P56)[tt]
                        for ncol in range(2):
                            kw = dict(inc=sPE) if (ncol == 1 and tt == 1) else {}
                            r = em.op("tensor", "matmul",
                                      pd[:, 512 * ncol:512 * (ncol + 1)],
                                      hp[:, 256 * hcl + 128 * tt:
                                         256 * hcl + 128 * (tt + 1)],
                                      WdT[:, E * hc + 512 * ncol:
                                          E * hc + 512 * (ncol + 1)],
                                      start=(hc == 0), stop=(hc == 23), **kw)
                    gt["down"][(q6, hcl)] = r
            gt["ffn_gu"] = sPE.n
            if sup + 1 < n_t // 2:
                emit_ldxnf(sup + 1)
            gt["ffn_mul_last"] = gt["mul"][-1]
            gt["ffn_silu_last"] = gt["silu"][-1]
            gt["down_done"] = sPE.n
            if b_sub < 4:
                continue
            for tt, td in ((0, t0), (1, t1)):
                pd = (P23, P56)[tt]
                em.wait("scalar", sPE, gt["down_done"])
                prev_ov = gt.get("ov_last") if tt == 1 else                     (g[t0 - 1].get("ov_last") if t0 >= 1 else None)
                if prev_ov:
                    em.wait("scalar", sDV, prev_ov)
                gk = em.op("scalar", "activation", g_bf[:, :], pd.ap(), AF.Silu,
                           scale=SC2, inc=sAC)
                g[td]["g_mark"] = gk
                if debug:
                    em.wait("gpsimd", sAC, gk)
                    dv_ = em.dma("gpsimd", dbg_g[td], g_bf[:, :], sGP)
                    em.wait("scalar", sGP, dv_)
                em.wait("vector", sAC, gk)
                for hf in range(2):
                    ob = out_sb[hf]
                    if td >= 1 or tt == 1:
                        em.wait("vector", sGP, g[td - 1]["st_y"] if tt == 0 else 0)
                    prev = gt.get("last_st_y_" + str(hf))
                    if prev:
                        em.wait("vector", sGP, prev)
                    em.wait("vector", sDV, gt["sbf_sup"])
                    ov = em.op("vector", "tensor_tensor", ob[:, :],
                               s_sup[:, 1024 * tt + 512 * hf:1024 * tt + 512 * (hf + 1)],
                               g_bf[:, 512 * hf:512 * (hf + 1)],
                               op=ALU.mult, inc=sDV)
                    gt["ov_last"] = ov
                    g[td]["ov_last"] = ov
                    em.wait("gpsimd", sDV, ov)
                    av = em.dma("gpsimd", ob[:, :], x3[td][:, 512 * hf:512 * (hf + 1)],
                                sGP, accum_op=ALU.add)
                    em.wait("gpsimd", sGP, av)
                    stv = em.dma("gpsimd", y3[td][:, 512 * hf:512 * (hf + 1)], ob[:, :], sGP)
                    gt["last_st_y_" + str(hf)] = stv
                g[td]["st_y"] = stv

    em.wait("gpsimd", sGP, sGP.n)
    return _finish(nc, em)


def _finish(nc, em):
    with nc.Block() as block:
        @block.sync
        def _(eng):
            for f in em.ops["sync"]:
                f(eng)

        @block.gpsimd
        def _(eng):
            for f in em.ops["gpsimd"]:
                f(eng)

        @block.scalar
        def _(eng):
            for f in em.ops["scalar"]:
                f(eng)

        @block.vector
        def _(eng):
            for f in em.ops["vector"]:
                f(eng)

        @block.tensor
        def _(eng):
            for f in em.ops["tensor"]:
                f(eng)

    return nc


_NC_CACHE = {}


def _get_nc(debug=False):
    if debug not in _NC_CACHE:
        _NC_CACHE[debug] = build(debug)
    return _NC_CACHE[debug]


def make_inmaps(L0_attn_x, L0_attn_qkv_w, L0_attn_qkv_b, L0_attn_out_w, L0_attn_out_b,
                L0_ffn_w_gate, L0_ffn_w_up, L0_ffn_w_down):
    bf = ml_dtypes.bfloat16
    ident = np.eye(128, dtype=bf)
    mask4 = np.tile(np.triu(np.ones((128, 128), dtype=bf)), (1, 4))
    ones1 = np.ones((128, 1), dtype=bf)
    ones2 = np.ones((2, 128), dtype=bf)
    wqkv = np.ascontiguousarray(np.asarray(L0_attn_qkv_w, dtype=np.float32))
    bqkv = np.ascontiguousarray(
        np.asarray(L0_attn_qkv_b, dtype=np.float32).reshape(3, E))
    wout = np.ascontiguousarray(np.asarray(L0_attn_out_w, dtype=np.float32))
    bout = np.ascontiguousarray(
        np.asarray(L0_attn_out_b, dtype=np.float32).reshape(1, E))
    wg = np.ascontiguousarray(np.asarray(L0_ffn_w_gate, dtype=np.int32))
    wu = np.ascontiguousarray(np.asarray(L0_ffn_w_up, dtype=np.int32))
    wd = np.ascontiguousarray(np.asarray(L0_ffn_w_down, dtype=np.int32))
    x = np.asarray(L0_attn_x, dtype=np.float32)
    in_maps = []
    for core in range(NCORE):
        b = core // 4
        s0 = (core % 4) * TOK
        mv = np.zeros((64, 4), np.float32)
        mv[:, :core % 4] = 1.0
        in_maps.append({
            "x": np.ascontiguousarray(x[b, s0:s0 + TOK, :]),
            "wqkv": wqkv, "bqkv": bqkv, "wout": wout, "bout": bout,
            "wg": wg, "wu": wu, "wd": wd,
            "ident": ident, "mask4": mask4, "ones1": ones1, "ones2": ones2,
            "maskv": mv,
        })
    return in_maps


def kernel(L0_attn_x, L0_attn_qkv_w, L0_attn_qkv_b, L0_attn_out_w, L0_attn_out_b,
           L0_ffn_w_gate, L0_ffn_w_up, L0_ffn_w_down):
    nc = _get_nc(False)
    in_maps = make_inmaps(L0_attn_x, L0_attn_qkv_w, L0_attn_qkv_b, L0_attn_out_w,
                          L0_attn_out_b, L0_ffn_w_gate, L0_ffn_w_up, L0_ffn_w_down)
    res = run_bass_kernel_spmd(nc, in_maps, core_ids=list(range(NCORE)))
    out = np.empty((B, L, E), dtype=np.float32)
    for core in range(NCORE):
        b = core // 4
        s0 = (core % 4) * TOK
        out[b, s0:s0 + TOK, :] = np.asarray(res.results[core]["y"])
    return out
